# version 18
# baseline (speedup 1.0000x reference)
import numpy as np
import ml_dtypes
import concourse.bass as bass
import concourse.mybir as mybir
from concourse.bass_utils import run_bass_kernel_spmd

F32 = mybir.dt.float32
BF16 = mybir.dt.bfloat16
U32 = mybir.dt.uint32
AF = mybir.ActivationFunctionType
ALU = mybir.AluOpType

D = 1024
HID = 4096
NEXP = 8
NCORES = 8
TOK_PER_CORE = 4096
NT = TOK_PER_CORE // 128
EPS = 1e-5

ENGINES = ("sync", "gpsimd", "vector", "scalar", "tensor")


class Sch:

    def __init__(self):
        self.streams = {e: [] for e in ENGINES}
        self.ecnt = {e: 0 for e in ENGINES}
        self.dcnt = {}
        self.floors = {e: {} for e in ENGINES}
        self.bufs = {}
        self.dma_rr = {"sync": 0, "gpsimd": 0, "scalar": 0, "vector": 0}
        self.npool = {"sync": 10, "gpsimd": 10, "scalar": 6, "vector": 6}

    def _need(self, eng, key, val):
        f = self.floors[eng]
        if f.get(key, 0) >= val:
            return
        f[key] = val
        self.streams[eng].append(("w", key, val))

    def _hazards(self, eng, reads, writes):
        for b in reads:
            s = self.bufs.get(b)
            if s:
                for key, val in s["w"]:
                    self._need(eng, key, val)
        for b in writes:
            s = self.bufs.get(b)
            if s:
                for key, val in s["w"]:
                    self._need(eng, key, val)
                for key, val in s["r"]:
                    self._need(eng, key, val)

    def _record(self, done, reads, writes):
        for b in writes:
            self.bufs[b] = {"w": [done], "r": []}
        for b in reads:
            s = self.bufs.setdefault(b, {"w": [], "r": []})
            s["r"].append(done)

    def op(self, eng, emit, reads=(), writes=()):
        self._hazards(eng, reads, writes)
        self.ecnt[eng] += 1
        key = ("e", eng)
        self.streams[eng].append(("i", emit, key, 1))
        self._record((key, self.ecnt[eng]), reads, writes)

    def group(self, eng, emits, reads=(), writes=()):
        assert emits
        self._hazards(eng, reads, writes)
        for e in emits[:-1]:
            self.streams[eng].append(("i", e, None, 0))
        self.ecnt[eng] += 1
        key = ("e", eng)
        self.streams[eng].append(("i", emits[-1], key, 1))
        self._record((key, self.ecnt[eng]), reads, writes)

    def dma(self, eng, emit, reads=(), writes=()):
        slot = self.dma_rr[eng] % self.npool[eng]
        self.dma_rr[eng] += 1
        key = ("d", eng, slot)
        prev = self.dcnt.get(key, 0)
        if prev:
            self._need(eng, key, prev)
        self._hazards(eng, reads, writes)
        self.dcnt[key] = prev + 16
        self.streams[eng].append(("i", emit, key, 16))
        self._record((key, prev + 16), reads, writes)

    def finish(self):
        for key, tgt in sorted(self.dcnt.items()):
            self._need(key[1], key, tgt)


def build_nc(ntiles=NT, dbg=False):
    nc = bass.Bass("TRN2", target_bir_lowering=False, debug=False,
                   dynamic_dma_scratch_size=16384)

    x_in = nc.dram_tensor("x", [ntiles * 128, D], F32, kind="ExternalInput").ap()
    out = nc.dram_tensor("out", [ntiles * 128, D], F32, kind="ExternalOutput").ap()

    def win(name, shape):
        return nc.dram_tensor(name, shape, F32, kind="ExternalInput").ap()

    i_w1 = win("proj_wt", [D, D])
    i_w2 = win("fc1_wt", [D, HID])
    i_w3 = win("fc2_wt", [HID, D])
    i_we = win("ffn_W", [NEXP, D, D])
    i_sig = win("sigst", [128, 8 * NEXP])
    i_g1c = win("g1c_i", [128, 8])
    i_b1c = win("b1c_i", [128, 8])
    i_g2c = win("g2c_i", [128, 8])
    i_b2c = win("b2c_i", [128, 8])
    i_ghc = win("ghc_i", [128, 32])
    i_bhc = win("bhc_i", [128, 32])
    i_gm1bc = win("gm1bc_i", [128, D])
    i_gm2bc = win("gm2bc_i", [128, D])
    i_pb = win("proj_b", [D])
    i_f1b = win("fc1_b", [HID])
    i_f2b = win("fc2_b", [D])
    i_be = win("ffn_b", [NEXP, D])
    i_ident = nc.dram_tensor("ident", [128, 128], BF16, kind="ExternalInput").ap()
    i_iota = nc.dram_tensor("iota", [128, NEXP], F32, kind="ExternalInput").ap()

    scr_x1 = nc.dram_tensor("scr_x1", [ntiles * 128, D], BF16, kind="Internal").ap()
    scr_xft = nc.dram_tensor("scr_xft", [ntiles * 128, D], BF16, kind="Internal").ap()
    scr_we = nc.dram_tensor("scr_we", [64 * 128, D], BF16, kind="Internal").ap()

    from contextlib import ExitStack
    ctx = ExitStack()
    _carves = []
    GUARD = 2048

    def carve(name, shape, dtype, off):
        h = nc.virtual_sbuf_tensor(name, list(shape), dtype)
        _carves.append((h, list(shape), dtype, off + GUARD))
        return h

    def finalize_arena(total):
        total += GUARD
        arena_h = ctx.enter_context(
            nc.sbuf_tensor("arena", [128, total], mybir.dt.uint8))
        base = nc.lookup_mloc(arena_h).addr
        assert base + total <= 229344, (base, total)
        nc._carve_info = {}
        for h, shape, dtype, off in _carves:
            nc._tensor(h.name, shape, dtype, type="SB")
            mloc = nc.lookup_mloc(h)
            mloc.allocated = not nc.target_bir_lowering
            mloc.addr = base + off
            nc._carve_info[h.name] = (off, shape, dtype)

    w1t = carve("w1t", [128, 8 * D], BF16, 0)
    w2t = carve("w2t", [128, 8 * HID], BF16, 16 * 1024)
    w3t = carve("w3t", [128, 32 * D], BF16, 80 * 1024)
    we = carve("we", [128, 64 * D], BF16, 0)
    OFF = 144 * 1024

    def nxt(nbytes, align=64):
        nonlocal OFF
        OFF = (OFF + align - 1) // align * align
        o = OFF
        OFF += nbytes
        return o

    ident = carve("identb", [128, 128], BF16, nxt(256))
    iota = carve("iotab", [128, NEXP], F32, nxt(32))
    signt = carve("signt", [128, 8 * NEXP], BF16, nxt(128))
    ballb = carve("ballb", [NEXP, D], BF16, nxt(2048))
    mstt = carve("mstt", [NEXP, 128], BF16, nxt(256))
    mbf = carve("mbf", [128, NEXP], BF16, nxt(16))
    scm = carve("scm", [128, 3 * NEXP], F32, nxt(3 * NEXP * 4))
    scmu = carve("scmu", [128, NEXP], U32, nxt(NEXP * 4))
    masks = carve("masks", [128, NT * NEXP], F32, nxt(NT * NEXP * 4))
    hstat = carve("hstat", [128, 96], F32, nxt(96 * 4))
    g1c = carve("g1c", [128, 8], F32, nxt(32))
    g2c = carve("g2c", [128, 8], F32, nxt(32))
    ghc = carve("ghc", [128, 32], F32, nxt(128))
    b1c = carve("b1c", [128, 8], BF16, nxt(16))
    b2c = carve("b2c", [128, 8], BF16, nxt(16))
    bhc = carve("bhc", [128, 32], BF16, nxt(64))
    bsm = carve("bsm", [128, 48], F32, nxt(192))
    onesrow = carve("onesrow", [1, 128], BF16, nxt(256))
    onescol = carve("onescol", [128, 1], BF16, nxt(16))
    c3row = carve("c3row", [1, D], BF16, nxt(2048))
    rows2 = carve("rows2", [2, D], BF16, nxt(2048))
    arows2 = carve("arows2", [2, 128], BF16, nxt(256))
    acol2 = carve("acol2", [128, 2], BF16, nxt(16))
    westg = [carve(f"westg{k}", [128, 512], F32, nxt(2048)) for k in range(2)]
    webf = [carve(f"webf{k}", [128, 512], BF16, nxt(1024)) for k in range(2)]
    stats = [carve("stat0", [128, 16], F32, nxt(64)),
             carve("stat1", [128, 16], F32, nxt(64))]
    smalls = [carve("sml0", [128, 16], F32, nxt(64)),
              carve("sml1", [128, 16], F32, nxt(64))]
    stats2 = [carve("st2a", [128, 16], F32, nxt(64)),
              carve("st2b", [128, 16], F32, nxt(64))]
    smalls2 = [carve("sm2a", [128, 16], F32, nxt(64)),
               carve("sm2b", [128, 16], F32, nxt(64))]

    c1row = carve("c1row", [1, D], BF16, nxt(2048))
    c2row = carve("c2row", [1, HID], BF16, nxt(8192))

    def anxt(nbytes, align=64):
        nonlocal aoff
        aoff = (aoff + align - 1) // align * align
        o = aoff
        aoff += nbytes
        return o
    aoff = OFF

    A0 = (OFF + 63) // 64 * 64
    aoff = A0
    stg = [carve(f"stg{k}", [128, 1024], F32, anxt(4096)) for k in range(3)]
    gm1bc = carve("gm1bc", [128, D], F32, anxt(4096))
    gm2bc = carve("gm2bc", [128, D], F32, anxt(4096))
    prow = carve("prow", [1, 2048], F32, anxt(8192))
    prow2 = carve("prow2", [1, D], F32, anxt(4096))
    PREP_BUFS = ["stg0", "stg1", "stg2", "gm1bc", "gm2bc", "prow", "prow2"]
    A_PREP_END = aoff
    aoff = A0
    x0s = [carve(f"x0s{k}", [128, D], F32, anxt(4096)) for k in range(2)]
    ub = carve("ub", [128, D], BF16, anxt(2048))
    xfb = carve("xfb", [128, D], BF16, anxt(2048))
    tTz = carve("tTz", [128, D], BF16, anxt(2048))
    xhts = [carve(f"xht{k}", [128, D], BF16, anxt(2048)) for k in range(2)]
    x2b = carve("x2b", [128, D], BF16, anxt(2048))
    zs = [carve("zs0", [128, D], BF16, anxt(2048))]
    hgs = [carve(f"hgq{k}", [128, D], BF16, anxt(2048)) for k in range(2)]
    sbuf_s = [carve(f"ss{k}", [128, D], BF16, anxt(2048)) for k in range(2)]
    x1s = [carve("x1s0", [128, D], BF16, anxt(2048))]
    xfts = [carve("xft0", [128, D], BF16, anxt(2048))]
    PG1_BUFS = ["x0s0", "x0s1", "ub", "xfb", "tTz", "xht0", "xht1",
                "x2b", "zs0", "hgq0", "hgq1", "ss0", "ss1",
                "x1s0", "xft0"]
    A_PG2_END = aoff
    aoff = A0
    accf = [carve(f"acc{k}", [128, D], F32, anxt(4096)) for k in range(2)]
    rxft = [carve(f"rxft{k}", [128, D], BF16, anxt(2048)) for k in range(2)]
    rx1b = [carve(f"rx1b{k}", [128, D], BF16, anxt(2048)) for k in range(2)]
    PG3_BUFS = ["acc0", "acc1", "rxft0", "rxft1", "rx1b0", "rx1b1"]
    A_PG3_END = aoff
    OFF = max(A_PREP_END, A_PG2_END, A_PG3_END)
    finalize_arena(OFF)

    pA = ctx.enter_context(nc.psum_tensor("pA", [128, D], F32))
    pB = ctx.enter_context(nc.psum_tensor("pB", [128, D], F32))
    pC = ctx.enter_context(nc.psum_tensor("pC", [128, D], F32))
    pD = ctx.enter_context(nc.psum_tensor("pD", [128, 512], F32))
    pE = ctx.enter_context(nc.psum_tensor("pE", [128, 512], F32))

    sch = Sch()

    def DMA(eng, dst, src, reads, writes):
        sch.dma(eng, (lambda e, d=dst, s=src: e.dma_start(out=d, in_=s)),
                reads, writes)

    _tp = [0]

    def transpose_tile(src_h, src_buf, dst_h, dst_buf, nch, banks=None,
                       evac="vector"):
        for g0 in range(0, nch, 4):
            gn = min(4, nch - g0)
            if banks is None:
                pT, c0, bT = (pE, 0, "pE") if _tp[0] % 2 == 0 else (pD, 0, "pD")
                _tp[0] += 1
            else:
                pT, c0, bT = banks[(g0 // 4) % len(banks)]
            mms = []
            for j in range(gn):
                c = g0 + j
                mms.append(lambda e, c=c, j=j, pT=pT, c0=c0: e.matmul(
                    pT[:, c0 + j * 128:c0 + (j + 1) * 128],
                    src_h.ap()[:, c * 128:(c + 1) * 128],
                    ident.ap(), start=True, stop=True, skip_group_check=True))
            sch.group("tensor", mms, reads=(src_buf, "ident"), writes=(bT,))
            if evac == "scalar":
                sch.op("scalar",
                       (lambda e, g0=g0, gn=gn, pT=pT, c0=c0: e.activation(
                           out=dst_h.ap()[:, g0 * 128:(g0 + gn) * 128],
                           in_=pT[:, c0:c0 + gn * 128], func=AF.Identity)),
                       reads=(bT,), writes=(dst_buf,))
            else:
                sch.op("vector",
                       (lambda e, g0=g0, gn=gn, pT=pT, c0=c0: e.tensor_copy(
                           out=dst_h.ap()[:, g0 * 128:(g0 + gn) * 128],
                           in_=pT[:, c0:c0 + gn * 128])),
                       reads=(bT,), writes=(dst_buf,))

    DMA("sync", ident.ap()[:], i_ident, (), ("ident",))
    DMA("sync", iota.ap()[:], i_iota, (), ("iota",))
    DMA("sync", g1c.ap()[:], i_g1c, (), ("g1c",))
    DMA("sync", g2c.ap()[:], i_g2c, (), ("g2c",))
    DMA("sync", ghc.ap()[:], i_ghc, (), ("ghc",))
    DMA("sync", gm1bc.ap()[:], i_gm1bc, (), ("gm1bc",))
    DMA("sync", gm2bc.ap()[:], i_gm2bc, (), ("gm2bc",))
    DMA("sync", bsm.ap()[:, 0:8], i_b1c, (), ("bsm1",))
    DMA("sync", bsm.ap()[:, 8:16], i_b2c, (), ("bsm2",))
    DMA("sync", bsm.ap()[:, 16:48], i_bhc, (), ("bsm3",))
    sch.op("vector", (lambda e: e.tensor_copy(out=b1c.ap()[:],
                                              in_=bsm.ap()[:, 0:8])),
           reads=("bsm1",), writes=("b1c",))
    sch.op("vector", (lambda e: e.tensor_copy(out=b2c.ap()[:],
                                              in_=bsm.ap()[:, 8:16])),
           reads=("bsm2",), writes=("b2c",))
    sch.op("vector", (lambda e: e.tensor_copy(out=bhc.ap()[:],
                                              in_=bsm.ap()[:, 16:48])),
           reads=("bsm3",), writes=("bhc",))
    sch.op("vector", (lambda e: e.memset(onesrow.ap()[:], 1.0)),
           reads=(), writes=("onesrow",))
    DMA("sync", stg[0].ap()[:, 0:64], i_sig, (), ("stg0",))
    sch.op("scalar", (lambda e: e.activation(out=signt.ap()[:],
                                             in_=stg[0].ap()[:, 0:64],
                                             func=AF.Sign)),
           reads=("stg0",), writes=("signt",))
    DMA("sync", stg[1].ap()[0:NEXP, 0:D], i_be, (), ("stg1",))
    sch.op("vector", (lambda e: e.tensor_copy(out=ballb.ap()[:],
                                              in_=stg[1].ap()[0:NEXP, 0:D])),
           reads=("stg1",), writes=("ballb",))

    _j = [0]

    def ldconv(src2d, r0, cols_src, dst_ap, dst_buf, fold_bc=None,
               fold_buf=None):
        k = _j[0] % 3
        _j[0] += 1
        w = cols_src.stop - cols_src.start
        DMA(("sync", "gpsimd", "scalar")[k],
            stg[k].ap()[:, 0:w], src2d[r0:r0 + 128, cols_src],
            (), (f"stg{k}",))
        if fold_bc is not None:
            sch.op("vector", (lambda e, k=k, w=w: e.tensor_tensor(
                out=dst_ap, in0=stg[k].ap()[:, 0:w],
                in1=fold_bc, op=ALU.mult)),
                reads=(f"stg{k}", fold_buf), writes=(dst_buf,))
        else:
            sch.op("vector", (lambda e, k=k, w=w: e.tensor_copy(
                out=dst_ap, in_=stg[k].ap()[:, 0:w])),
                reads=(f"stg{k}",), writes=(dst_buf,))

    for ic in range(8):
        ldconv(i_w1, ic * 128, slice(0, D),
               w1t.ap()[:, ic * D:(ic + 1) * D], f"w1t{ic}",
               fold_bc=gm1bc.ap()[:], fold_buf="gm1bc")
    DMA("sync", prow.ap()[0:1, 0:D], i_pb.unsqueeze(0), (), ("prow",))
    sch.op("vector", (lambda e: e.tensor_tensor(
        out=prow2.ap()[0:1, 0:D], in0=gm1bc.ap()[0:1, 0:D],
        in1=prow.ap()[0:1, 0:D], op=ALU.mult)),
        reads=("gm1bc", "prow"), writes=("prow2",))
    for half in range(2):
        mms = [(lambda e, ic=ic, half=half: e.matmul(
            pD[0:1, 0:512], b1c.ap()[:, ic:ic + 1],
            w1t.ap()[:, ic * D + half * 512: ic * D + (half + 1) * 512],
            start=(ic == 0), stop=(ic == 7), skip_group_check=True))
            for ic in range(8)]
        sch.group("tensor", mms,
                  reads=tuple(f"w1t{ic}" for ic in range(8)) + ("b1c",),
                  writes=("pD",))
        sch.op("vector", (lambda e, half=half: e.tensor_tensor(
            out=c1row.ap()[0:1, half * 512:(half + 1) * 512],
            in0=pD[0:1, 0:512],
            in1=prow2.ap()[0:1, half * 512:(half + 1) * 512], op=ALU.add)),
            reads=("pD", "prow2"), writes=("c1row",))
    for ic in range(8):
        sch.op("vector", (lambda e, ic=ic: e.tensor_scalar(
            out=w1t.ap()[:, ic * D:(ic + 1) * D],
            in0=w1t.ap()[:, ic * D:(ic + 1) * D],
            scalar1=g1c.ap()[:, ic:ic + 1], scalar2=None, op0=ALU.mult)),
            reads=(f"w1t{ic}", "g1c"), writes=(f"w1t{ic}",))

    for ic in range(8):
        for qtr in range(4):
            ldconv(i_w2, ic * 128, slice(qtr * 1024, (qtr + 1) * 1024),
                   w2t.ap()[:, ic * HID + qtr * 1024:
                            ic * HID + (qtr + 1) * 1024], f"w2t{ic}")
    for grp in range(2):
        DMA("sync", prow.ap()[0:1, 0:2048],
            i_f1b[grp * 2048:(grp + 1) * 2048].unsqueeze(0),
            ("prow",), ("prow",))
        for sl in range(4):
            asl = grp * 4 + sl
            mms = [(lambda e, ic=ic, asl=asl: e.matmul(
                pD[0:1, 0:512], b2c.ap()[:, ic:ic + 1],
                w2t.ap()[:, ic * HID + asl * 512: ic * HID + (asl + 1) * 512],
                start=(ic == 0), stop=(ic == 7), skip_group_check=True))
                for ic in range(8)]
            sch.group("tensor", mms,
                      reads=tuple(f"w2t{ic}" for ic in range(8)) + ("b2c",),
                      writes=("pD",))
            sch.op("vector", (lambda e, sl=sl, asl=asl: e.tensor_tensor(
                out=c2row.ap()[0:1, asl * 512:(asl + 1) * 512],
                in0=pD[0:1, 0:512],
                in1=prow.ap()[0:1, sl * 512:(sl + 1) * 512], op=ALU.add)),
                reads=("pD", "prow"), writes=("c2row",))
    for ic in range(8):
        sch.op("vector", (lambda e, ic=ic: e.tensor_scalar(
            out=w2t.ap()[:, ic * HID:(ic + 1) * HID],
            in0=w2t.ap()[:, ic * HID:(ic + 1) * HID],
            scalar1=g2c.ap()[:, ic:ic + 1], scalar2=None, op0=ALU.mult)),
            reads=(f"w2t{ic}", "g2c"), writes=(f"w2t{ic}",))

    for hc in range(32):
        ldconv(i_w3, hc * 128, slice(0, D),
               w3t.ap()[:, hc * D:(hc + 1) * D], f"w3t{hc}",
               fold_bc=gm2bc.ap()[:], fold_buf="gm2bc")
    DMA("sync", prow.ap()[0:1, 0:D], i_f2b.unsqueeze(0), ("prow",), ("prow",))
    sch.op("vector", (lambda e: e.tensor_tensor(
        out=prow2.ap()[0:1, 0:D], in0=gm2bc.ap()[0:1, 0:D],
        in1=prow.ap()[0:1, 0:D], op=ALU.mult)),
        reads=("gm2bc", "prow"), writes=("prow2",))
    for half in range(2):
        mms = [(lambda e, hc=hc, half=half: e.matmul(
            pD[0:1, 0:512], bhc.ap()[:, hc:hc + 1],
            w3t.ap()[:, hc * D + half * 512: hc * D + (half + 1) * 512],
            start=(hc == 0), stop=(hc == 31), skip_group_check=True))
            for hc in range(32)]
        sch.group("tensor", mms,
                  reads=tuple(f"w3t{hc}" for hc in range(32)) + ("bhc",),
                  writes=("pD",))
        sch.op("vector", (lambda e, half=half: e.tensor_tensor(
            out=c3row.ap()[0:1, half * 512:(half + 1) * 512],
            in0=pD[0:1, 0:512],
            in1=prow2.ap()[0:1, half * 512:(half + 1) * 512], op=ALU.add)),
            reads=("pD", "prow2"), writes=("c3row",))
    for hc in range(32):
        sch.op("vector", (lambda e, hc=hc: e.tensor_scalar(
            out=w3t.ap()[:, hc * D:(hc + 1) * D],
            in0=w3t.ap()[:, hc * D:(hc + 1) * D],
            scalar1=ghc.ap()[:, hc:hc + 1], scalar2=None, op0=ALU.mult)),
            reads=(f"w3t{hc}", "ghc"), writes=(f"w3t{hc}",))

    sch.op("vector", (lambda e: e.memset(onescol.ap()[:], 1.0)),
           reads=(), writes=("onescol",))
    for half in range(2):
        mms = [(lambda e, hc=hc, half=half: e.matmul(
            pD[0:1, 0:512], onescol.ap()[:],
            w3t.ap()[:, hc * D + half * 512: hc * D + (half + 1) * 512],
            start=(hc == 0), stop=(hc == 31), skip_group_check=True))
            for hc in range(32)]
        sch.group("tensor", mms,
                  reads=tuple(f"w3t{hc}" for hc in range(32)) + ("onescol",),
                  writes=("pD",))
        sch.op("vector", (lambda e, half=half: e.tensor_copy(
            out=rows2.ap()[0:1, half * 512:(half + 1) * 512],
            in_=pD[0:1, 0:512])),
            reads=("pD",), writes=("rows2",))
    DMA("sync", rows2.ap()[1:2, 0:D], c3row.ap()[0:1, 0:D],
        ("c3row",), ("rows2",))

    if dbg:
        def dump(name, src_ap, shape, dtype, reads):
            d = nc.dram_tensor(name, shape, dtype, kind="ExternalOutput").ap()
            DMA("sync", d, src_ap, reads, (name,))
        dump("d_w1t", w1t.ap()[:], [128, 8 * D], BF16,
             tuple(f"w1t{ic}" for ic in range(8)))
        dump("d_w2t", w2t.ap()[:], [128, 8 * HID], BF16,
             tuple(f"w2t{ic}" for ic in range(8)))
        dump("d_w3t", w3t.ap()[:], [128, 32 * D], BF16,
             tuple(f"w3t{hc}" for hc in range(32)))
        dump("d_c1row", c1row.ap()[:], [1, D], BF16, ("c1row",))
        dump("d_c2row", c2row.ap()[:], [1, HID], BF16, ("c2row",))
        dump("d_c3row", c3row.ap()[:], [1, D], BF16, ("c3row",))
        dump("d_signt", signt.ap()[:], [128, 8 * NEXP], BF16, ("signt",))

    def pg1_load(i):
        kx = i % 2
        DMA("gpsimd", x0s[kx].ap()[:], x_in[i * 128:(i + 1) * 128, :],
            (), (f"x0s{kx}",))

    def uchain(i):
        k = i % 2
        st = stats[k].ap()
        sm = smalls[k].ap()
        X = x0s[k].ap()
        bx = f"x0s{k}"
        sch.op("vector", (lambda e: e.bn_stats(out=st[:, 2:8], in_=X[:, 0:512])),
               reads=(bx,), writes=(f"st{k}",))
        sch.op("vector", (lambda e: e.bn_stats(out=st[:, 8:14], in_=X[:, 512:1024])),
               reads=(bx,), writes=(f"st{k}",))
        sch.op("vector", (lambda e: e.bn_aggr(out=st[:, 0:2], in_=st[:, 2:14])),
               reads=(f"st{k}",), writes=(f"st{k}",))
        sch.op("vector", (lambda e: e.tensor_scalar(
            out=sm[:, 0:1], in0=st[:, 1:2], scalar1=1.0, scalar2=EPS,
            op0=ALU.mult, op1=ALU.add)), reads=(f"st{k}",), writes=(f"sm{k}",))
        sch.op("vector", (lambda e: e.tensor_scalar(
            out=sm[:, 1:2], in0=st[:, 1:2], scalar1=1.0 + EPS, scalar2=EPS * EPS,
            op0=ALU.mult, op1=ALU.add)), reads=(f"st{k}",), writes=(f"sm{k}",))
        sch.op("vector", (lambda e: e.reciprocal(out=sm[:, 2:4], in_=sm[:, 0:2])),
               reads=(f"sm{k}",), writes=(f"sm{k}",))
        sch.op("scalar", (lambda e: e.activation(out=sm[:, 4:6], in_=sm[:, 2:4],
                                                 func=AF.Sqrt)),
               reads=(f"sm{k}",), writes=(f"sm{k}",))
        sch.op("vector", (lambda e: e.reciprocal(out=sm[:, 7:8], in_=sm[:, 5:6])),
               reads=(f"sm{k}",), writes=(f"sm{k}",))
        sch.op("vector", (lambda e: e.tensor_scalar(
            out=sm[:, 6:7], in0=sm[:, 4:5], scalar1=sm[:, 7:8], scalar2=None,
            op0=ALU.mult)), reads=(f"sm{k}",), writes=(f"sm{k}",))
        sch.op("vector", (lambda e: e.tensor_scalar(
            out=sm[:, 8:9], in0=st[:, 0:1], scalar1=sm[:, 5:6], scalar2=-1.0,
            op0=ALU.mult, op1=ALU.mult)),
            reads=(f"st{k}", f"sm{k}"), writes=(f"sm{k}",))
        sch.op("scalar", (lambda e: e.activation(
            out=ub.ap()[:], in_=X[:], func=AF.Identity, bias=sm[:, 8:9],
            scale=sm[:, 5:6])), reads=(bx, f"sm{k}"), writes=("ub",))

    def uT_proj(i):
        transpose_tile(ub, "ub", xfts[0], "xft0", 8)
        for half in range(2):
            mms = [(lambda e, ic=ic, half=half: e.matmul(
                pC[:, half * 512:(half + 1) * 512],
                xfts[0].ap()[:, ic * 128:(ic + 1) * 128],
                w1t.ap()[:, ic * D + half * 512: ic * D + (half + 1) * 512],
                start=(ic == 0), stop=False, skip_group_check=True))
                for ic in range(8)]
            mms.append(lambda e, half=half: e.matmul(
                pC[:, half * 512:(half + 1) * 512], onesrow.ap()[:],
                c1row.ap()[0:1, half * 512:(half + 1) * 512],
                start=False, stop=True, skip_group_check=True))
            sch.group("tensor", mms,
                      reads=("xft0", "onesrow", "c1row") +
                      tuple(f"w1t{ic}" for ic in range(8)),
                      writes=("pC",))

    def x2z_v(i):
        k = i % 2
        st = stats[k].ap()
        sm = smalls[k].ap()
        X = x0s[k].ap()
        bx = f"x0s{k}"
        X2 = x2b.ap()
        sch.op("vector", (lambda e: e.scalar_tensor_tensor(
            out=X2[:], in0=ub.ap()[:], scalar=sm[:, 6:7], in1=pC[:],
            op0=ALU.mult, op1=ALU.add)),
            reads=("ub", f"sm{k}", "pC"), writes=("x2b",))
        SS = sbuf_s[k].ap()
        sch.op("vector", (lambda e: e.tensor_tensor(
            out=SS[:], in0=X2[:], in1=X[:], op=ALU.add)),
            reads=("x2b", bx), writes=(f"ss{k}",))
        sch.op("vector", (lambda e: e.bn_stats(out=st[:, 2:8], in_=X2[:, 0:512])),
               reads=("x2b",), writes=(f"st2{k}",))
        sch.op("vector", (lambda e: e.bn_stats(out=st[:, 8:14], in_=X2[:, 512:1024])),
               reads=("x2b",), writes=(f"st2{k}",))
        sch.op("vector", (lambda e: e.bn_aggr(out=st[:, 0:2], in_=st[:, 2:14])),
               reads=(f"st2{k}",), writes=(f"st2{k}",))
        sch.op("vector", (lambda e: e.tensor_scalar(
            out=sm[:, 9:10], in0=st[:, 1:2], scalar1=1.0, scalar2=EPS,
            op0=ALU.mult, op1=ALU.add)), reads=(f"st2{k}",), writes=(f"sm2{k}",))
        sch.op("vector", (lambda e: e.reciprocal(out=sm[:, 10:11], in_=sm[:, 9:10])),
               reads=(f"sm2{k}",), writes=(f"sm2{k}",))

    def x2z_s(i):
        k = i % 2
        st = stats[k].ap()
        sm = smalls[k].ap()
        X2 = x2b.ap()
        sch.op("scalar", (lambda e: e.activation(out=sm[:, 11:12], in_=sm[:, 10:11],
                                                 func=AF.Sqrt)),
               reads=(f"sm2{k}",), writes=(f"sm2{k}",))
        sch.op("vector", (lambda e: e.tensor_scalar(
            out=sm[:, 12:13], in0=st[:, 0:1], scalar1=sm[:, 11:12], scalar2=-1.0,
            op0=ALU.mult, op1=ALU.mult)),
            reads=(f"st2{k}", f"sm2{k}"), writes=(f"sm2{k}",))
        sch.op("scalar", (lambda e: e.activation(
            out=zs[0].ap()[:], in_=X2[:], func=AF.Identity, bias=sm[:, 12:13],
            scale=sm[:, 11:12])), reads=("x2b", f"sm2{k}"), writes=("zs0",))

    def zT(i):
        transpose_tile(zs[0], "zs0", tTz, "tTz", 8)

    def fc1q(i, q):
        k = i % 2
        h2 = q % 2
        pq, bq = (pC, "pC") if q == 1 else (pB, "pB")
        for half in range(2):
            mms = [(lambda e, ic=ic, q=q, half=half, pq=pq: e.matmul(
                pq[:, half * 512:(half + 1) * 512],
                tTz.ap()[:, ic * 128:(ic + 1) * 128],
                w2t.ap()[:, ic * HID + q * 1024 + half * 512:
                         ic * HID + q * 1024 + (half + 1) * 512],
                start=(ic == 0), stop=False, skip_group_check=True))
                for ic in range(8)]
            mms.append(lambda e, q=q, half=half, pq=pq: e.matmul(
                pq[:, half * 512:(half + 1) * 512], onesrow.ap()[:],
                c2row.ap()[0:1, q * 1024 + half * 512:
                           q * 1024 + (half + 1) * 512],
                start=False, stop=True, skip_group_check=True))
            sch.group("tensor", mms,
                      reads=("tTz", "onesrow", "c2row") +
                      tuple(f"w2t{ic}" for ic in range(8)),
                      writes=(bq,))
        HG = hgs[h2].ap()
        sch.op("scalar", (lambda e, pq=pq, HG=HG: e.activation(
            out=HG[:], in_=pq[:], func=AF.Gelu)),
            reads=(bq,), writes=(f"hgq{h2}",))
        sch.op("vector", (lambda e, HG=HG, k=k, q=q: e.bn_stats(
            out=hstat.ap()[:, k * 48 + q * 12:k * 48 + q * 12 + 6],
            in_=HG[:, 0:512])),
            reads=(f"hgq{h2}",), writes=(f"hstat{k}",))
        sch.op("vector", (lambda e, HG=HG, k=k, q=q: e.bn_stats(
            out=hstat.ap()[:, k * 48 + q * 12 + 6:k * 48 + q * 12 + 12],
            in_=HG[:, 512:1024])),
            reads=(f"hgq{h2}",), writes=(f"hstat{k}",))

    def hTq(i, q):
        h2 = q % 2
        transpose_tile(hgs[h2], f"hgq{h2}", xhts[h2], f"xht{h2}", 8,
                       evac="vector")

    def fc2q(i, q):
        h2 = q % 2
        for half in range(2):
            mms = [(lambda e, c=c, q=q, half=half, h2=h2: e.matmul(
                pA[:, half * 512:(half + 1) * 512],
                xhts[h2].ap()[:, c * 128:(c + 1) * 128],
                w3t.ap()[:, (q * 8 + c) * D + half * 512:
                         (q * 8 + c) * D + (half + 1) * 512],
                start=(q == 0 and c == 0), stop=False,
                skip_group_check=True))
                for c in range(8)]
            sch.group("tensor", mms,
                      reads=(f"xht{h2}",) + tuple(
                          f"w3t{q * 8 + c}" for c in range(8)),
                      writes=("pA",))

    def lnhfin(i):
        k = i % 2
        st = stats2[k].ap()
        sm = smalls2[k].ap()
        sch.op("vector", (lambda e, k=k: e.bn_aggr(
            out=st[:, 0:2], in_=hstat.ap()[:, k * 48:k * 48 + 48])),
            reads=(f"hstat{k}",), writes=(f"st2s{k}",))
        sch.op("vector", (lambda e: e.tensor_scalar(
            out=sm[:, 3:4], in0=st[:, 1:2], scalar1=1.0, scalar2=EPS,
            op0=ALU.mult, op1=ALU.add)), reads=(f"st2s{k}",),
            writes=(f"sm2s{k}",))
        sch.op("vector", (lambda e: e.reciprocal(out=sm[:, 6:7], in_=sm[:, 3:4])),
               reads=(f"sm2s{k}",), writes=(f"sm2s{k}",))
        sch.op("scalar", (lambda e: e.activation(out=sm[:, 4:5], in_=sm[:, 6:7],
                                                 func=AF.Sqrt)),
               reads=(f"sm2s{k}",), writes=(f"sm2s{k}",))
        sch.op("vector", (lambda e: e.tensor_scalar(
            out=acol2.ap()[:, 1:2], in0=sm[:, 3:4], scalar1=sm[:, 4:5],
            scalar2=None, op0=ALU.mult)),
            reads=(f"sm2s{k}",), writes=("acol2",))
        sch.op("vector", (lambda e: e.tensor_scalar(
            out=acol2.ap()[:, 0:1], in0=st[:, 0:1], scalar1=-1.0, scalar2=None,
            op0=ALU.mult)), reads=(f"st2s{k}",), writes=("acol2",))

    def inject(i):
        sch.group("tensor", [(lambda e: e.matmul(
            pD[0:2, 0:128], acol2.ap()[:, 0:2], ident.ap(),
            start=True, stop=True, skip_group_check=True))],
            reads=("acol2", "ident"), writes=("pD",))
        sch.op("vector", (lambda e: e.tensor_copy(out=arows2.ap()[:],
                                                  in_=pD[0:2, 0:128])),
               reads=("pD",), writes=("arows2",))
        for half in range(2):
            sch.group("tensor", [(lambda e, half=half: e.matmul(
                pA[:, half * 512:(half + 1) * 512], arows2.ap()[0:2, :],
                rows2.ap()[0:2, half * 512:(half + 1) * 512],
                start=False, stop=True, skip_group_check=True))],
                reads=("arows2", "rows2"), writes=("pA",))

    def x1xf(i):
        k = i % 2
        st = stats2[k].ap()
        sm = smalls2[k].ap()
        X1 = x1s[0].ap()
        sch.op("vector", (lambda e, k=k: e.scalar_tensor_tensor(
            out=X1[:], in0=pA[:], scalar=sm[:, 4:5], in1=sbuf_s[k].ap()[:],
            op0=ALU.mult, op1=ALU.add)),
            reads=("pA", f"sm2s{k}", f"ss{k}"), writes=("x1s0",))
        DMA("sync", scr_x1[i * 128:(i + 1) * 128, :], X1[:], ("x1s0",),
            (f"scr_x1{i}",))
        sch.op("vector", (lambda e: e.bn_stats(out=st[:, 2:8], in_=X1[:, 0:512])),
               reads=("x1s0",), writes=(f"st2s{k}",))
        sch.op("vector", (lambda e: e.bn_stats(out=st[:, 8:14], in_=X1[:, 512:1024])),
               reads=("x1s0",), writes=(f"st2s{k}",))
        sch.op("vector", (lambda e: e.bn_aggr(out=st[:, 0:2], in_=st[:, 2:14])),
               reads=(f"st2s{k}",), writes=(f"st2s{k}",))
        sch.op("vector", (lambda e: e.tensor_scalar(
            out=sm[:, 7:8], in0=st[:, 1:2], scalar1=1.0, scalar2=EPS,
            op0=ALU.mult, op1=ALU.add)), reads=(f"st2s{k}",), writes=(f"sm2s{k}",))
        sch.op("vector", (lambda e: e.reciprocal(out=sm[:, 8:9], in_=sm[:, 7:8])),
               reads=(f"sm2s{k}",), writes=(f"sm2s{k}",))
        sch.op("scalar", (lambda e: e.activation(out=sm[:, 9:10], in_=sm[:, 8:9],
                                                 func=AF.Sqrt)),
               reads=(f"sm2s{k}",), writes=(f"sm2s{k}",))
        sch.op("vector", (lambda e: e.tensor_scalar(
            out=sm[:, 10:11], in0=st[:, 0:1], scalar1=sm[:, 9:10], scalar2=-1.0,
            op0=ALU.mult, op1=ALU.mult)), reads=(f"st2s{k}", f"sm2s{k}"),
            writes=(f"sm2s{k}",))
        sch.op("scalar", (lambda e, k=k: e.activation(
            out=xfb.ap()[:], in_=X1[:], func=AF.Identity, bias=sm[:, 10:11],
            scale=sm[:, 9:10])), reads=("x1s0", f"sm2s{k}"), writes=("xfb",))

    def xfT_scores(i):
        transpose_tile(xfb, "xfb", xfts[0], "xft0", 8, evac="scalar")
        DMA("sync", scr_xft[i * 128:(i + 1) * 128, :], xfts[0].ap()[:],
            ("xft0",), (f"scr_xft{i}",))
        mms = [(lambda e, c=c: e.matmul(
            pD[:, 0:NEXP], xfts[0].ap()[:, c * 128:(c + 1) * 128],
            signt.ap()[:, c * NEXP:(c + 1) * NEXP],
            start=(c == 0), stop=(c == 7), skip_group_check=True))
            for c in range(8)]
        sch.group("tensor", mms, reads=("xft0", "signt"), writes=("pD",))
        sch.op("vector", (lambda e: e.tensor_copy(out=scm.ap()[:, 0:NEXP],
                                                  in_=pD[:, 0:NEXP])),
               reads=("pD",), writes=("scm",))
        sch.op("vector", (lambda e: e.max(out=scm.ap()[:, NEXP:2 * NEXP],
                                          in_=scm.ap()[:, 0:NEXP])),
               reads=("scm",), writes=("scm",))
        sch.op("vector", (lambda e: e.max_index(out=scmu.ap()[:],
                                                in_max=scm.ap()[:, NEXP:2 * NEXP],
                                                in_values=scm.ap()[:, 0:NEXP])),
               reads=("scm",), writes=("scmu",))
        sch.op("vector", (lambda e: e.tensor_copy(
            out=scm.ap()[:, 2 * NEXP:2 * NEXP + 1], in_=scmu.ap()[:, 0:1])),
            reads=("scmu",), writes=("scm",))
        sch.op("vector", (lambda e, i=i: e.tensor_scalar(
            out=masks.ap()[:, i * NEXP:(i + 1) * NEXP], in0=iota.ap()[:],
            scalar1=scm.ap()[:, 2 * NEXP:2 * NEXP + 1], scalar2=None,
            op0=ALU.is_equal)), reads=("scm", "iota"), writes=(f"masks{i}",))

    _wej = [0]

    def wecast(j):
        e_i, dc = j // 8, j % 8
        for half in range(2):
            k = _wej[0] % 2
            _wej[0] += 1
            DMA(("sync", "gpsimd")[k], westg[k].ap()[:],
                i_we[e_i][dc * 128:(dc + 1) * 128,
                          half * 512:(half + 1) * 512],
                (), (f"westg{k}",))
            sch.op("vector", (lambda e, k=k: e.tensor_copy(
                out=webf[k].ap()[:], in_=westg[k].ap()[:])),
                reads=(f"westg{k}",), writes=(f"webf{k}",))
            DMA(("gpsimd", "sync")[k],
                scr_we[j * 128:(j + 1) * 128, half * 512:(half + 1) * 512],
                webf[k].ap()[:], (f"webf{k}",), (f"scrwe{j}h{half}",))

    def we_load(lo, hi):
        for j in range(lo, hi):
            DMA(("sync", "gpsimd", "scalar")[j % 3],
                we.ap()[:, j * D:(j + 1) * D],
                scr_we[j * 128:(j + 1) * 128, :],
                (f"scrwe{j}h0", f"scrwe{j}h1"), (f"we{j}",))

    def pg3_a(i):
        k = i % 2
        XFT = rxft[k].ap()
        X1B = rx1b[k].ap()
        if dbg and i == 0:
            def dump(name, src_ap, shape, dtype, reads):
                d = nc.dram_tensor(name, shape, dtype,
                                   kind="ExternalOutput").ap()
                DMA("sync", d, src_ap, reads, (name,))
            dump("d_ballb", ballb.ap()[:], [NEXP, D], BF16, ("ballb",))
            dump("d_we0", we.ap()[:, 0:8 * D], [128, 8 * D], BF16,
                 tuple(f"we{j}" for j in range(8)))
        DMA("sync", XFT[:], scr_xft[i * 128:(i + 1) * 128, :], (f"scr_xft{i}",),
            (f"rxft{k}",))
        DMA("sync", X1B[:], scr_x1[i * 128:(i + 1) * 128, :], (f"scr_x1{i}",),
            (f"rx1b{k}",))
        if dbg and i == 1:
            def dump(name, src_ap, shape, dtype, reads):
                d = nc.dram_tensor(name, shape, dtype,
                                   kind="ExternalOutput").ap()
                DMA("sync", d, src_ap, reads, (name,))
            dump("d_xftr", rxft[0].ap()[:], [128, D], BF16, ("rxft0",))
            dump("d_x1br", rx1b[0].ap()[:], [128, D], BF16, ("rx1b0",))
        sch.op("vector", (lambda e, i=i: e.tensor_copy(
            out=mbf.ap()[:], in_=masks.ap()[:, i * NEXP:(i + 1) * NEXP])),
            reads=(f"masks{i}",), writes=("mbf",))
        sch.group("tensor", [(lambda e: e.matmul(
            pD[0:NEXP, 128:256], mbf.ap()[:], ident.ap(),
            start=True, stop=True, skip_group_check=True))],
            reads=("mbf", "ident"), writes=("pD",))
        sch.op("vector", (lambda e: e.tensor_copy(
            out=mstt.ap()[:], in_=pD[0:NEXP, 128:256])),
            reads=("pD",), writes=("mstt",))
        for half in range(2):
            sch.group("tensor", [(lambda e, half=half: e.matmul(
                pC[:, half * 512:(half + 1) * 512],
                mstt.ap()[0:NEXP, :],
                ballb.ap()[:, half * 512:(half + 1) * 512],
                start=True, stop=True, skip_group_check=True))],
                reads=("mstt", "ballb"), writes=("pC",))
        ACC = accf[k].ap()
        sch.op("vector", (lambda e: e.tensor_tensor(
            out=ACC[:], in0=pC[:], in1=X1B[:], op=ALU.add)),
            reads=("pC", f"rx1b{k}"), writes=(f"acc{k}",))

    def pg3_b(i):
        k = i % 2
        XFT = rxft[k].ap()
        ACC = accf[k].ap()
        for e_i in range(NEXP):
            pq = (pA, pB)[e_i % 2]
            bq = ("pA", "pB")[e_i % 2]
            for half in range(2):
                mms = [(lambda e, c=c, e_i=e_i, half=half, pq=pq: e.matmul(
                    pq[:, half * 512:(half + 1) * 512],
                    XFT[:, c * 128:(c + 1) * 128],
                    we.ap()[:, (e_i * 8 + c) * D + half * 512:
                            (e_i * 8 + c) * D + (half + 1) * 512],
                    start=(c == 0), stop=(c == 7), skip_group_check=True))
                    for c in range(8)]
                sch.group("tensor", mms,
                          reads=(f"rxft{k}",) + tuple(
                              f"we{e_i * 8 + c}" for c in range(8)),
                          writes=(bq,))
            sch.op("vector", (lambda e, pq=pq, e_i=e_i, i=i:
                   e.scalar_tensor_tensor(
                       out=ACC[:], in0=pq[:],
                       scalar=masks.ap()[:, i * NEXP + e_i:i * NEXP + e_i + 1],
                       in1=ACC[:], op0=ALU.mult, op1=ALU.add)),
                reads=(bq, f"masks{i}", f"acc{k}"), writes=(f"acc{k}",))
        DMA("gpsimd", out[i * 128:(i + 1) * 128, :], ACC[:], (f"acc{k}",),
            (f"out{i}",))

    def alias(newbufs, oldbufs):
        for nb in newbufs:
            s = sch.bufs.setdefault(nb, {"w": [], "r": []})
            for ob in oldbufs:
                os_ = sch.bufs.get(ob)
                if os_:
                    s["w"] = s["w"] + os_["w"] + os_["r"]

    alias(PG1_BUFS, PREP_BUFS)
    pg1_load(0)
    pg1_load(1)
    uchain(0)
    uT_proj(0)
    x2z_v(0)
    x2z_s(0)
    uchain(1)
    pg1_load(2)
    zT(0)
    for i in range(ntiles):
        if 1 <= i and i + 1 < ntiles:
            uchain(i + 1)
        fc1q(i, 0)
        fc1q(i, 1)
        hTq(i, 0)
        fc2q(i, 0)
        if i >= 1:
            xfT_scores(i - 1)
        fc1q(i, 2)
        hTq(i, 1)
        fc2q(i, 1)
        if i + 1 < ntiles:
            uT_proj(i + 1)
            x2z_v(i + 1)
        hTq(i, 2)
        fc1q(i, 3)
        fc2q(i, 2)
        hTq(i, 3)
        if i + 1 < ntiles:
            x2z_s(i + 1)
        fc2q(i, 3)
        lnhfin(i)
        inject(i)
        if i + 1 < ntiles:
            zT(i + 1)
        x1xf(i)
        wecast(2 * i)
        wecast(2 * i + 1)
        if i + 3 < ntiles:
            pg1_load(i + 3)
    xfT_scores(ntiles - 1)
    alias([f"we{j}" for j in range(0, 40)],
          [f"w1t{ic}" for ic in range(8)] +
          [f"w2t{ic}" for ic in range(8)])
    we_load(0, 40)
    alias([f"we{j}" for j in range(40, 64)],
          [f"w3t{hc}" for hc in range(32)])
    we_load(40, 64)
    alias(PG3_BUFS, PG1_BUFS)
    pg3_a(0)
    for i in range(ntiles - 1):
        pg3_a(i + 1)
        pg3_b(i)
    pg3_b(ntiles - 1)

    sch.finish()

    sems = {}
    semlist = []
    for e in ENGINES:
        sems[("e", e)] = nc.alloc_semaphore(name=f"se_{e}")
        semlist.append(sems[("e", e)])
    for key in sch.dcnt:
        sems[key] = nc.alloc_semaphore(name=f"sd_{key[1]}{key[2]}")
        semlist.append(sems[key])

    with nc.Block() as block:
        def runner(eng_name):
            def _run(engobj):
                for item in sch.streams[eng_name]:
                    if item[0] == "w":
                        engobj.wait_ge(sems[item[1]], item[2])
                    else:
                        _, emit, key, amt = item
                        inst = emit(engobj)
                        if key is not None:
                            inst.then_inc(sems[key], amt)
            return _run

        block.sync(runner("sync"))
        block.gpsimd(runner("gpsimd"))
        block.vector(runner("vector"))
        block.scalar(runner("scalar"))
        block.tensor(runner("tensor"))

    for s in semlist:
        nc.release_semaphore(s)
    ctx.close()
    nc._sch = sch
    return nc


_NC_CACHE = {}


def _get_nc(ntiles=NT):
    if ntiles not in _NC_CACHE:
        _NC_CACHE[ntiles] = build_nc(ntiles)
    return _NC_CACHE[ntiles]


def host_layout(inputs):
    f = np.float32

    def a(v):
        return np.ascontiguousarray(np.asarray(v, dtype=f))
    chunks = lambda v, n: np.ascontiguousarray(
        np.asarray(v, dtype=f).reshape(n, 128).T)
    m = {
        "proj_wt": a(np.asarray(inputs["L0_attn_proj_w"]).T),
        "fc1_wt": a(np.asarray(inputs["L0_attn_fc1_w"]).T),
        "fc2_wt": a(np.asarray(inputs["L0_attn_fc2_w"]).T),
        "ffn_W": a(inputs["L0_ffn_W"]),
        "sigst": np.ascontiguousarray(
            np.asarray(inputs["L0_ffn_tile_sigs"], dtype=f).T
            .reshape(8, 128, NEXP).transpose(1, 0, 2).reshape(128, 8 * NEXP)),
        "g1c_i": chunks(inputs["L0_attn_ln1_g"], 8),
        "b1c_i": chunks(inputs["L0_attn_ln1_b"], 8),
        "g2c_i": chunks(inputs["L0_attn_ln2_g"], 8),
        "b2c_i": chunks(inputs["L0_attn_ln2_b"], 8),
        "ghc_i": chunks(inputs["L0_attn_lnh_g"], 32),
        "bhc_i": chunks(inputs["L0_attn_lnh_b"], 32),
        "gm1bc_i": np.ascontiguousarray(np.broadcast_to(
            np.asarray(inputs["L0_attn_gamma1"], dtype=f), (128, D))),
        "gm2bc_i": np.ascontiguousarray(np.broadcast_to(
            np.asarray(inputs["L0_attn_gamma2"], dtype=f), (128, D))),
        "proj_b": a(inputs["L0_attn_proj_b"]),
        "fc1_b": a(inputs["L0_attn_fc1_b"]),
        "fc2_b": a(inputs["L0_attn_fc2_b"]),
        "ffn_b": a(inputs["L0_ffn_b"]),
        "ident": np.eye(128, dtype=ml_dtypes.bfloat16),
        "iota": np.broadcast_to(
            np.arange(NEXP, dtype=f), (128, NEXP)).copy(),
    }
    return m


def kernel(**inputs):
    x = np.ascontiguousarray(np.asarray(inputs["L0_attn_x"], dtype=np.float32))
    B, S, Dm = x.shape
    xf = x.reshape(B * S, Dm)
    nc = _get_nc(NT)
    shared = host_layout(inputs)
    in_maps = []
    for c in range(NCORES):
        m = dict(shared)
        m["x"] = np.ascontiguousarray(
            xf[c * TOK_PER_CORE:(c + 1) * TOK_PER_CORE])
        in_maps.append(m)
    res = run_bass_kernel_spmd(nc, in_maps, core_ids=list(range(NCORES)))
    outs = [res.results[c]["out"] for c in range(NCORES)]
    y = np.concatenate(outs, axis=0).reshape(B, S, Dm).astype(np.float32)
    return y


# version 19 (final)
# speedup vs baseline: 1.1206x; 1.1206x over previous
import numpy as np
import ml_dtypes
import concourse.bass as bass
import concourse.mybir as mybir
from concourse.bass_utils import run_bass_kernel_spmd

F32 = mybir.dt.float32
BF16 = mybir.dt.bfloat16
U32 = mybir.dt.uint32
AF = mybir.ActivationFunctionType
ALU = mybir.AluOpType

D = 1024
HID = 4096
NEXP = 8
NCORES = 8
TOK_PER_CORE = 4096
NT = TOK_PER_CORE // 128
EPS = 1e-5

ENGINES = ("sync", "gpsimd", "vector", "scalar", "tensor")


class Sch:

    def __init__(self):
        self.streams = {e: [] for e in ENGINES}
        self.ecnt = {e: 0 for e in ENGINES}
        self.dcnt = {}
        self.floors = {e: {} for e in ENGINES}
        self.bufs = {}
        self.dma_rr = {"sync": 0, "gpsimd": 0, "scalar": 0, "vector": 0}
        self.npool = {"sync": 10, "gpsimd": 10, "scalar": 6, "vector": 6}

    def _need(self, eng, key, val):
        f = self.floors[eng]
        if f.get(key, 0) >= val:
            return
        f[key] = val
        self.streams[eng].append(("w", key, val))

    def _hazards(self, eng, reads, writes):
        for b in reads:
            s = self.bufs.get(b)
            if s:
                for key, val in s["w"]:
                    self._need(eng, key, val)
        for b in writes:
            s = self.bufs.get(b)
            if s:
                for key, val in s["w"]:
                    self._need(eng, key, val)
                for key, val in s["r"]:
                    self._need(eng, key, val)

    def _record(self, done, reads, writes):
        for b in writes:
            self.bufs[b] = {"w": [done], "r": []}
        for b in reads:
            s = self.bufs.setdefault(b, {"w": [], "r": []})
            s["r"].append(done)

    def op(self, eng, emit, reads=(), writes=()):
        self._hazards(eng, reads, writes)
        self.ecnt[eng] += 1
        key = ("e", eng)
        self.streams[eng].append(("i", emit, key, 1))
        self._record((key, self.ecnt[eng]), reads, writes)

    def group(self, eng, emits, reads=(), writes=()):
        assert emits
        self._hazards(eng, reads, writes)
        for e in emits[:-1]:
            self.streams[eng].append(("i", e, None, 0))
        self.ecnt[eng] += 1
        key = ("e", eng)
        self.streams[eng].append(("i", emits[-1], key, 1))
        self._record((key, self.ecnt[eng]), reads, writes)

    def dma(self, eng, emit, reads=(), writes=()):
        slot = self.dma_rr[eng] % self.npool[eng]
        self.dma_rr[eng] += 1
        key = ("d", eng, slot)
        prev = self.dcnt.get(key, 0)
        if prev:
            self._need(eng, key, prev)
        self._hazards(eng, reads, writes)
        self.dcnt[key] = prev + 16
        self.streams[eng].append(("i", emit, key, 16))
        self._record((key, prev + 16), reads, writes)

    def finish(self):
        for key, tgt in sorted(self.dcnt.items()):
            self._need(key[1], key, tgt)


def build_nc(ntiles=NT, dbg=False):
    nc = bass.Bass("TRN2", target_bir_lowering=False, debug=False,
                   dynamic_dma_scratch_size=16384)

    x_in = nc.dram_tensor("x", [ntiles * 128, D], F32, kind="ExternalInput").ap()
    out = nc.dram_tensor("out", [ntiles * 128, D], F32, kind="ExternalOutput").ap()

    def win(name, shape):
        return nc.dram_tensor(name, shape, F32, kind="ExternalInput").ap()

    i_w1 = win("proj_wt", [D, D])
    i_w2 = win("fc1_wt", [D, HID])
    i_w3 = win("fc2_wt", [HID, D])
    i_we = win("ffn_W", [NEXP, D, D])
    i_sig = win("sigst", [128, 8 * NEXP])
    i_g1c = win("g1c_i", [128, 8])
    i_b1c = win("b1c_i", [128, 8])
    i_g2c = win("g2c_i", [128, 8])
    i_b2c = win("b2c_i", [128, 8])
    i_ghc = win("ghc_i", [128, 32])
    i_bhc = win("bhc_i", [128, 32])
    i_gm1bc = win("gm1bc_i", [128, D])
    i_gm2bc = win("gm2bc_i", [128, D])
    i_pb = win("proj_b", [D])
    i_f1b = win("fc1_b", [HID])
    i_f2b = win("fc2_b", [D])
    i_be = win("ffn_b", [NEXP, D])
    i_ident = nc.dram_tensor("ident", [128, 128], BF16, kind="ExternalInput").ap()
    i_iota = nc.dram_tensor("iota", [128, NEXP], F32, kind="ExternalInput").ap()

    scr_x1 = nc.dram_tensor("scr_x1", [ntiles * 128, D], BF16, kind="Internal").ap()
    scr_xft = nc.dram_tensor("scr_xft", [ntiles * 128, D], BF16, kind="Internal").ap()

    from contextlib import ExitStack
    ctx = ExitStack()
    _carves = []
    GUARD = 2048

    def carve(name, shape, dtype, off):
        h = nc.virtual_sbuf_tensor(name, list(shape), dtype)
        _carves.append((h, list(shape), dtype, off + GUARD))
        return h

    def finalize_arena(total):
        total += GUARD
        arena_h = ctx.enter_context(
            nc.sbuf_tensor("arena", [128, total], mybir.dt.uint8))
        base = nc.lookup_mloc(arena_h).addr
        assert base + total <= 229344, (base, total)
        nc._carve_info = {}
        for h, shape, dtype, off in _carves:
            nc._tensor(h.name, shape, dtype, type="SB")
            mloc = nc.lookup_mloc(h)
            mloc.allocated = not nc.target_bir_lowering
            mloc.addr = base + off
            nc._carve_info[h.name] = (off, shape, dtype)

    w1t = carve("w1t", [128, 8 * D], BF16, 0)
    w2t = carve("w2t", [128, 8 * HID], BF16, 16 * 1024)
    w3t = carve("w3t", [128, 32 * D], BF16, 80 * 1024)
    we = carve("we", [128, 64 * D], BF16, 0)
    OFF = 144 * 1024

    def nxt(nbytes, align=64):
        nonlocal OFF
        OFF = (OFF + align - 1) // align * align
        o = OFF
        OFF += nbytes
        return o

    ident = carve("identb", [128, 128], BF16, nxt(256))
    iota = carve("iotab", [128, NEXP], F32, nxt(32))
    signt = carve("signt", [128, 8 * NEXP], BF16, nxt(128))
    ballb = carve("ballb", [NEXP, D], BF16, nxt(2048))
    mstt = carve("mstt", [NEXP, 128], BF16, nxt(256))
    mbf = carve("mbf", [128, NEXP], BF16, nxt(16))
    scm = carve("scm", [128, 3 * NEXP], F32, nxt(3 * NEXP * 4))
    scmu = carve("scmu", [128, NEXP], U32, nxt(NEXP * 4))
    masks = carve("masks", [128, NT * NEXP], F32, nxt(NT * NEXP * 4))
    hstat = carve("hstat", [128, 96], F32, nxt(96 * 4))
    g1c = carve("g1c", [128, 8], F32, nxt(32))
    g2c = carve("g2c", [128, 8], F32, nxt(32))
    ghc = carve("ghc", [128, 32], F32, nxt(128))
    b1c = carve("b1c", [128, 8], BF16, nxt(16))
    b2c = carve("b2c", [128, 8], BF16, nxt(16))
    bhc = carve("bhc", [128, 32], BF16, nxt(64))
    bsm = carve("bsm", [128, 48], F32, nxt(192))
    onesrow = carve("onesrow", [1, 128], BF16, nxt(256))
    onescol = carve("onescol", [128, 1], BF16, nxt(16))
    c3row = carve("c3row", [1, D], BF16, nxt(2048))
    rows2 = carve("rows2", [2, D], BF16, nxt(2048))
    arows2 = carve("arows2", [2, 128], BF16, nxt(256))
    acol2 = carve("acol2", [128, 2], BF16, nxt(16))
    westg = [carve(f"westg{k}", [128, 512], F32, nxt(2048)) for k in range(3)]
    stats = [carve("stat0", [128, 16], F32, nxt(64)),
             carve("stat1", [128, 16], F32, nxt(64))]
    smalls = [carve("sml0", [128, 16], F32, nxt(64)),
              carve("sml1", [128, 16], F32, nxt(64))]
    stats2 = [carve("st2a", [128, 16], F32, nxt(64)),
              carve("st2b", [128, 16], F32, nxt(64))]
    smalls2 = [carve("sm2a", [128, 16], F32, nxt(64)),
               carve("sm2b", [128, 16], F32, nxt(64))]

    c1row = carve("c1row", [1, D], BF16, nxt(2048))
    c2row = carve("c2row", [1, HID], BF16, nxt(8192))

    def anxt(nbytes, align=64):
        nonlocal aoff
        aoff = (aoff + align - 1) // align * align
        o = aoff
        aoff += nbytes
        return o
    aoff = OFF

    A0 = (OFF + 63) // 64 * 64
    aoff = A0
    stg = [carve(f"stg{k}", [128, 1024], F32, anxt(4096)) for k in range(3)]
    gm1bc = carve("gm1bc", [128, D], F32, anxt(4096))
    gm2bc = carve("gm2bc", [128, D], F32, anxt(4096))
    prow = carve("prow", [1, 2048], F32, anxt(8192))
    prow2 = carve("prow2", [1, D], F32, anxt(4096))
    PREP_BUFS = ["stg0", "stg1", "stg2", "gm1bc", "gm2bc", "prow", "prow2"]
    A_PREP_END = aoff
    aoff = A0
    x0s = [carve(f"x0s{k}", [128, D], F32, anxt(4096)) for k in range(2)]
    ub = carve("ub", [128, D], BF16, anxt(2048))
    xfb = carve("xfb", [128, D], BF16, anxt(2048))
    tTz = carve("tTz", [128, D], BF16, anxt(2048))
    xhts = [carve(f"xht{k}", [128, D], BF16, anxt(2048)) for k in range(2)]
    x2b = carve("x2b", [128, D], BF16, anxt(2048))
    zs = [carve("zs0", [128, D], BF16, anxt(2048))]
    hgs = [carve(f"hgq{k}", [128, D], BF16, anxt(2048)) for k in range(2)]
    sbuf_s = [carve(f"ss{k}", [128, D], BF16, anxt(2048)) for k in range(2)]
    x1s = [carve("x1s0", [128, D], BF16, anxt(2048))]
    xfts = [carve("xft0", [128, D], BF16, anxt(2048))]
    PG1_BUFS = ["x0s0", "x0s1", "ub", "xfb", "tTz", "xht0", "xht1",
                "x2b", "zs0", "hgq0", "hgq1", "ss0", "ss1",
                "x1s0", "xft0"]
    A_PG2_END = aoff
    aoff = A0
    accf = [carve(f"acc{k}", [128, D], F32, anxt(4096)) for k in range(2)]
    rxft = [carve(f"rxft{k}", [128, D], BF16, anxt(2048)) for k in range(2)]
    rx1b = [carve(f"rx1b{k}", [128, D], BF16, anxt(2048)) for k in range(2)]
    PG3_BUFS = ["acc0", "acc1", "rxft0", "rxft1", "rx1b0", "rx1b1"]
    A_PG3_END = aoff
    OFF = max(A_PREP_END, A_PG2_END, A_PG3_END)
    finalize_arena(OFF)

    pA = ctx.enter_context(nc.psum_tensor("pA", [128, D], F32))
    pB = ctx.enter_context(nc.psum_tensor("pB", [128, D], F32))
    pC = ctx.enter_context(nc.psum_tensor("pC", [128, D], F32))
    pD = ctx.enter_context(nc.psum_tensor("pD", [128, 512], F32))
    pE = ctx.enter_context(nc.psum_tensor("pE", [128, 512], F32))

    sch = Sch()

    def DMA(eng, dst, src, reads, writes):
        sch.dma(eng, (lambda e, d=dst, s=src: e.dma_start(out=d, in_=s)),
                reads, writes)

    _tp = [0]

    def transpose_tile(src_h, src_buf, dst_h, dst_buf, nch, banks=None,
                       evac="vector"):
        for g0 in range(0, nch, 4):
            gn = min(4, nch - g0)
            if banks is None:
                pT, c0, bT = (pE, 0, "pE") if _tp[0] % 2 == 0 else (pD, 0, "pD")
                _tp[0] += 1
            else:
                pT, c0, bT = banks[(g0 // 4) % len(banks)]
            mms = []
            for j in range(gn):
                c = g0 + j
                mms.append(lambda e, c=c, j=j, pT=pT, c0=c0: e.matmul(
                    pT[:, c0 + j * 128:c0 + (j + 1) * 128],
                    src_h.ap()[:, c * 128:(c + 1) * 128],
                    ident.ap(), start=True, stop=True, skip_group_check=True))
            sch.group("tensor", mms, reads=(src_buf, "ident"), writes=(bT,))
            if evac == "scalar":
                sch.op("scalar",
                       (lambda e, g0=g0, gn=gn, pT=pT, c0=c0: e.activation(
                           out=dst_h.ap()[:, g0 * 128:(g0 + gn) * 128],
                           in_=pT[:, c0:c0 + gn * 128], func=AF.Identity)),
                       reads=(bT,), writes=(dst_buf,))
            else:
                sch.op("vector",
                       (lambda e, g0=g0, gn=gn, pT=pT, c0=c0: e.tensor_copy(
                           out=dst_h.ap()[:, g0 * 128:(g0 + gn) * 128],
                           in_=pT[:, c0:c0 + gn * 128])),
                       reads=(bT,), writes=(dst_buf,))

    DMA("sync", ident.ap()[:], i_ident, (), ("ident",))
    DMA("sync", iota.ap()[:], i_iota, (), ("iota",))
    DMA("sync", g1c.ap()[:], i_g1c, (), ("g1c",))
    DMA("sync", g2c.ap()[:], i_g2c, (), ("g2c",))
    DMA("sync", ghc.ap()[:], i_ghc, (), ("ghc",))
    DMA("sync", gm1bc.ap()[:], i_gm1bc, (), ("gm1bc",))
    DMA("sync", gm2bc.ap()[:], i_gm2bc, (), ("gm2bc",))
    DMA("sync", bsm.ap()[:, 0:8], i_b1c, (), ("bsm1",))
    DMA("sync", bsm.ap()[:, 8:16], i_b2c, (), ("bsm2",))
    DMA("sync", bsm.ap()[:, 16:48], i_bhc, (), ("bsm3",))
    sch.op("vector", (lambda e: e.tensor_copy(out=b1c.ap()[:],
                                              in_=bsm.ap()[:, 0:8])),
           reads=("bsm1",), writes=("b1c",))
    sch.op("vector", (lambda e: e.tensor_copy(out=b2c.ap()[:],
                                              in_=bsm.ap()[:, 8:16])),
           reads=("bsm2",), writes=("b2c",))
    sch.op("vector", (lambda e: e.tensor_copy(out=bhc.ap()[:],
                                              in_=bsm.ap()[:, 16:48])),
           reads=("bsm3",), writes=("bhc",))
    sch.op("vector", (lambda e: e.memset(onesrow.ap()[:], 1.0)),
           reads=(), writes=("onesrow",))
    DMA("sync", stg[0].ap()[:, 0:64], i_sig, (), ("stg0",))
    sch.op("scalar", (lambda e: e.activation(out=signt.ap()[:],
                                             in_=stg[0].ap()[:, 0:64],
                                             func=AF.Sign)),
           reads=("stg0",), writes=("signt",))
    DMA("sync", stg[1].ap()[0:NEXP, 0:D], i_be, (), ("stg1",))
    sch.op("vector", (lambda e: e.tensor_copy(out=ballb.ap()[:],
                                              in_=stg[1].ap()[0:NEXP, 0:D])),
           reads=("stg1",), writes=("ballb",))

    _j = [0]

    def ldconv(src2d, r0, cols_src, dst_ap, dst_buf, fold_bc=None,
               fold_buf=None):
        k = _j[0] % 3
        _j[0] += 1
        w = cols_src.stop - cols_src.start
        DMA(("sync", "gpsimd", "scalar")[k],
            stg[k].ap()[:, 0:w], src2d[r0:r0 + 128, cols_src],
            (), (f"stg{k}",))
        if fold_bc is not None:
            sch.op("vector", (lambda e, k=k, w=w: e.tensor_tensor(
                out=dst_ap, in0=stg[k].ap()[:, 0:w],
                in1=fold_bc, op=ALU.mult)),
                reads=(f"stg{k}", fold_buf), writes=(dst_buf,))
        else:
            sch.op("vector", (lambda e, k=k, w=w: e.tensor_copy(
                out=dst_ap, in_=stg[k].ap()[:, 0:w])),
                reads=(f"stg{k}",), writes=(dst_buf,))

    for ic in range(8):
        ldconv(i_w1, ic * 128, slice(0, D),
               w1t.ap()[:, ic * D:(ic + 1) * D], f"w1t{ic}",
               fold_bc=gm1bc.ap()[:], fold_buf="gm1bc")
    DMA("sync", prow.ap()[0:1, 0:D], i_pb.unsqueeze(0), (), ("prow",))
    sch.op("vector", (lambda e: e.tensor_tensor(
        out=prow2.ap()[0:1, 0:D], in0=gm1bc.ap()[0:1, 0:D],
        in1=prow.ap()[0:1, 0:D], op=ALU.mult)),
        reads=("gm1bc", "prow"), writes=("prow2",))
    for half in range(2):
        mms = [(lambda e, ic=ic, half=half: e.matmul(
            pD[0:1, 0:512], b1c.ap()[:, ic:ic + 1],
            w1t.ap()[:, ic * D + half * 512: ic * D + (half + 1) * 512],
            start=(ic == 0), stop=(ic == 7), skip_group_check=True))
            for ic in range(8)]
        sch.group("tensor", mms,
                  reads=tuple(f"w1t{ic}" for ic in range(8)) + ("b1c",),
                  writes=("pD",))
        sch.op("vector", (lambda e, half=half: e.tensor_tensor(
            out=c1row.ap()[0:1, half * 512:(half + 1) * 512],
            in0=pD[0:1, 0:512],
            in1=prow2.ap()[0:1, half * 512:(half + 1) * 512], op=ALU.add)),
            reads=("pD", "prow2"), writes=("c1row",))
    for ic in range(8):
        sch.op("vector", (lambda e, ic=ic: e.tensor_scalar(
            out=w1t.ap()[:, ic * D:(ic + 1) * D],
            in0=w1t.ap()[:, ic * D:(ic + 1) * D],
            scalar1=g1c.ap()[:, ic:ic + 1], scalar2=None, op0=ALU.mult)),
            reads=(f"w1t{ic}", "g1c"), writes=(f"w1t{ic}",))

    for ic in range(8):
        for qtr in range(4):
            ldconv(i_w2, ic * 128, slice(qtr * 1024, (qtr + 1) * 1024),
                   w2t.ap()[:, ic * HID + qtr * 1024:
                            ic * HID + (qtr + 1) * 1024], f"w2t{ic}")
    for grp in range(2):
        DMA("sync", prow.ap()[0:1, 0:2048],
            i_f1b[grp * 2048:(grp + 1) * 2048].unsqueeze(0),
            ("prow",), ("prow",))
        for sl in range(4):
            asl = grp * 4 + sl
            mms = [(lambda e, ic=ic, asl=asl: e.matmul(
                pD[0:1, 0:512], b2c.ap()[:, ic:ic + 1],
                w2t.ap()[:, ic * HID + asl * 512: ic * HID + (asl + 1) * 512],
                start=(ic == 0), stop=(ic == 7), skip_group_check=True))
                for ic in range(8)]
            sch.group("tensor", mms,
                      reads=tuple(f"w2t{ic}" for ic in range(8)) + ("b2c",),
                      writes=("pD",))
            sch.op("vector", (lambda e, sl=sl, asl=asl: e.tensor_tensor(
                out=c2row.ap()[0:1, asl * 512:(asl + 1) * 512],
                in0=pD[0:1, 0:512],
                in1=prow.ap()[0:1, sl * 512:(sl + 1) * 512], op=ALU.add)),
                reads=("pD", "prow"), writes=("c2row",))
    for ic in range(8):
        sch.op("vector", (lambda e, ic=ic: e.tensor_scalar(
            out=w2t.ap()[:, ic * HID:(ic + 1) * HID],
            in0=w2t.ap()[:, ic * HID:(ic + 1) * HID],
            scalar1=g2c.ap()[:, ic:ic + 1], scalar2=None, op0=ALU.mult)),
            reads=(f"w2t{ic}", "g2c"), writes=(f"w2t{ic}",))

    for hc in range(32):
        ldconv(i_w3, hc * 128, slice(0, D),
               w3t.ap()[:, hc * D:(hc + 1) * D], f"w3t{hc}",
               fold_bc=gm2bc.ap()[:], fold_buf="gm2bc")
    DMA("sync", prow.ap()[0:1, 0:D], i_f2b.unsqueeze(0), ("prow",), ("prow",))
    sch.op("vector", (lambda e: e.tensor_tensor(
        out=prow2.ap()[0:1, 0:D], in0=gm2bc.ap()[0:1, 0:D],
        in1=prow.ap()[0:1, 0:D], op=ALU.mult)),
        reads=("gm2bc", "prow"), writes=("prow2",))
    for half in range(2):
        mms = [(lambda e, hc=hc, half=half: e.matmul(
            pD[0:1, 0:512], bhc.ap()[:, hc:hc + 1],
            w3t.ap()[:, hc * D + half * 512: hc * D + (half + 1) * 512],
            start=(hc == 0), stop=(hc == 31), skip_group_check=True))
            for hc in range(32)]
        sch.group("tensor", mms,
                  reads=tuple(f"w3t{hc}" for hc in range(32)) + ("bhc",),
                  writes=("pD",))
        sch.op("vector", (lambda e, half=half: e.tensor_tensor(
            out=c3row.ap()[0:1, half * 512:(half + 1) * 512],
            in0=pD[0:1, 0:512],
            in1=prow2.ap()[0:1, half * 512:(half + 1) * 512], op=ALU.add)),
            reads=("pD", "prow2"), writes=("c3row",))
    for hc in range(32):
        sch.op("vector", (lambda e, hc=hc: e.tensor_scalar(
            out=w3t.ap()[:, hc * D:(hc + 1) * D],
            in0=w3t.ap()[:, hc * D:(hc + 1) * D],
            scalar1=ghc.ap()[:, hc:hc + 1], scalar2=None, op0=ALU.mult)),
            reads=(f"w3t{hc}", "ghc"), writes=(f"w3t{hc}",))

    sch.op("vector", (lambda e: e.memset(onescol.ap()[:], 1.0)),
           reads=(), writes=("onescol",))
    for half in range(2):
        mms = [(lambda e, hc=hc, half=half: e.matmul(
            pD[0:1, 0:512], onescol.ap()[:],
            w3t.ap()[:, hc * D + half * 512: hc * D + (half + 1) * 512],
            start=(hc == 0), stop=(hc == 31), skip_group_check=True))
            for hc in range(32)]
        sch.group("tensor", mms,
                  reads=tuple(f"w3t{hc}" for hc in range(32)) + ("onescol",),
                  writes=("pD",))
        sch.op("vector", (lambda e, half=half: e.tensor_copy(
            out=rows2.ap()[0:1, half * 512:(half + 1) * 512],
            in_=pD[0:1, 0:512])),
            reads=("pD",), writes=("rows2",))
    DMA("sync", rows2.ap()[1:2, 0:D], c3row.ap()[0:1, 0:D],
        ("c3row",), ("rows2",))

    if dbg:
        def dump(name, src_ap, shape, dtype, reads):
            d = nc.dram_tensor(name, shape, dtype, kind="ExternalOutput").ap()
            DMA("sync", d, src_ap, reads, (name,))
        dump("d_w1t", w1t.ap()[:], [128, 8 * D], BF16,
             tuple(f"w1t{ic}" for ic in range(8)))
        dump("d_w2t", w2t.ap()[:], [128, 8 * HID], BF16,
             tuple(f"w2t{ic}" for ic in range(8)))
        dump("d_w3t", w3t.ap()[:], [128, 32 * D], BF16,
             tuple(f"w3t{hc}" for hc in range(32)))
        dump("d_c1row", c1row.ap()[:], [1, D], BF16, ("c1row",))
        dump("d_c2row", c2row.ap()[:], [1, HID], BF16, ("c2row",))
        dump("d_c3row", c3row.ap()[:], [1, D], BF16, ("c3row",))
        dump("d_signt", signt.ap()[:], [128, 8 * NEXP], BF16, ("signt",))

    def pg1_load(i):
        kx = i % 2
        DMA("gpsimd", x0s[kx].ap()[:], x_in[i * 128:(i + 1) * 128, :],
            (), (f"x0s{kx}",))

    def uchain(i):
        k = i % 2
        st = stats[k].ap()
        sm = smalls[k].ap()
        X = x0s[k].ap()
        bx = f"x0s{k}"
        sch.op("vector", (lambda e: e.bn_stats(out=st[:, 2:8], in_=X[:, 0:512])),
               reads=(bx,), writes=(f"st{k}",))
        sch.op("vector", (lambda e: e.bn_stats(out=st[:, 8:14], in_=X[:, 512:1024])),
               reads=(bx,), writes=(f"st{k}",))
        sch.op("vector", (lambda e: e.bn_aggr(out=st[:, 0:2], in_=st[:, 2:14])),
               reads=(f"st{k}",), writes=(f"st{k}",))
        sch.op("vector", (lambda e: e.tensor_scalar(
            out=sm[:, 0:1], in0=st[:, 1:2], scalar1=1.0, scalar2=EPS,
            op0=ALU.mult, op1=ALU.add)), reads=(f"st{k}",), writes=(f"sm{k}",))
        sch.op("vector", (lambda e: e.tensor_scalar(
            out=sm[:, 1:2], in0=st[:, 1:2], scalar1=1.0 + EPS, scalar2=EPS * EPS,
            op0=ALU.mult, op1=ALU.add)), reads=(f"st{k}",), writes=(f"sm{k}",))
        sch.op("vector", (lambda e: e.reciprocal(out=sm[:, 2:4], in_=sm[:, 0:2])),
               reads=(f"sm{k}",), writes=(f"sm{k}",))
        sch.op("scalar", (lambda e: e.activation(out=sm[:, 4:6], in_=sm[:, 2:4],
                                                 func=AF.Sqrt)),
               reads=(f"sm{k}",), writes=(f"sm{k}",))
        sch.op("vector", (lambda e: e.reciprocal(out=sm[:, 7:8], in_=sm[:, 5:6])),
               reads=(f"sm{k}",), writes=(f"sm{k}",))
        sch.op("vector", (lambda e: e.tensor_scalar(
            out=sm[:, 6:7], in0=sm[:, 4:5], scalar1=sm[:, 7:8], scalar2=None,
            op0=ALU.mult)), reads=(f"sm{k}",), writes=(f"sm{k}",))
        sch.op("vector", (lambda e: e.tensor_scalar(
            out=sm[:, 8:9], in0=st[:, 0:1], scalar1=sm[:, 5:6], scalar2=-1.0,
            op0=ALU.mult, op1=ALU.mult)),
            reads=(f"st{k}", f"sm{k}"), writes=(f"sm{k}",))
        sch.op("scalar", (lambda e: e.activation(
            out=ub.ap()[:], in_=X[:], func=AF.Identity, bias=sm[:, 8:9],
            scale=sm[:, 5:6])), reads=(bx, f"sm{k}"), writes=("ub",))

    def uT_proj(i):
        transpose_tile(ub, "ub", xfts[0], "xft0", 8)
        for half in range(2):
            mms = [(lambda e, ic=ic, half=half: e.matmul(
                pC[:, half * 512:(half + 1) * 512],
                xfts[0].ap()[:, ic * 128:(ic + 1) * 128],
                w1t.ap()[:, ic * D + half * 512: ic * D + (half + 1) * 512],
                start=(ic == 0), stop=False, skip_group_check=True))
                for ic in range(8)]
            mms.append(lambda e, half=half: e.matmul(
                pC[:, half * 512:(half + 1) * 512], onesrow.ap()[:],
                c1row.ap()[0:1, half * 512:(half + 1) * 512],
                start=False, stop=True, skip_group_check=True))
            sch.group("tensor", mms,
                      reads=("xft0", "onesrow", "c1row") +
                      tuple(f"w1t{ic}" for ic in range(8)),
                      writes=("pC",))

    def x2z_v(i):
        k = i % 2
        st = stats[k].ap()
        sm = smalls[k].ap()
        X = x0s[k].ap()
        bx = f"x0s{k}"
        X2 = x2b.ap()
        sch.op("vector", (lambda e: e.scalar_tensor_tensor(
            out=X2[:], in0=ub.ap()[:], scalar=sm[:, 6:7], in1=pC[:],
            op0=ALU.mult, op1=ALU.add)),
            reads=("ub", f"sm{k}", "pC"), writes=("x2b",))
        SS = sbuf_s[k].ap()
        sch.op("vector", (lambda e: e.tensor_tensor(
            out=SS[:], in0=X2[:], in1=X[:], op=ALU.add)),
            reads=("x2b", bx), writes=(f"ss{k}",))
        sch.op("vector", (lambda e: e.bn_stats(out=st[:, 2:8], in_=X2[:, 0:512])),
               reads=("x2b",), writes=(f"st2{k}",))
        sch.op("vector", (lambda e: e.bn_stats(out=st[:, 8:14], in_=X2[:, 512:1024])),
               reads=("x2b",), writes=(f"st2{k}",))
        sch.op("vector", (lambda e: e.bn_aggr(out=st[:, 0:2], in_=st[:, 2:14])),
               reads=(f"st2{k}",), writes=(f"st2{k}",))
        sch.op("vector", (lambda e: e.tensor_scalar(
            out=sm[:, 9:10], in0=st[:, 1:2], scalar1=1.0, scalar2=EPS,
            op0=ALU.mult, op1=ALU.add)), reads=(f"st2{k}",), writes=(f"sm2{k}",))
        sch.op("vector", (lambda e: e.reciprocal(out=sm[:, 10:11], in_=sm[:, 9:10])),
               reads=(f"sm2{k}",), writes=(f"sm2{k}",))

    def x2z_s(i):
        k = i % 2
        st = stats[k].ap()
        sm = smalls[k].ap()
        X2 = x2b.ap()
        sch.op("scalar", (lambda e: e.activation(out=sm[:, 11:12], in_=sm[:, 10:11],
                                                 func=AF.Sqrt)),
               reads=(f"sm2{k}",), writes=(f"sm2{k}",))
        sch.op("vector", (lambda e: e.tensor_scalar(
            out=sm[:, 12:13], in0=st[:, 0:1], scalar1=sm[:, 11:12], scalar2=-1.0,
            op0=ALU.mult, op1=ALU.mult)),
            reads=(f"st2{k}", f"sm2{k}"), writes=(f"sm2{k}",))
        sch.op("scalar", (lambda e: e.activation(
            out=zs[0].ap()[:], in_=X2[:], func=AF.Identity, bias=sm[:, 12:13],
            scale=sm[:, 11:12])), reads=("x2b", f"sm2{k}"), writes=("zs0",))

    def zT(i):
        transpose_tile(zs[0], "zs0", tTz, "tTz", 8)

    def fc1q(i, q):
        k = i % 2
        h2 = q % 2
        pq, bq = (pC, "pC") if q == 1 else (pB, "pB")
        for half in range(2):
            mms = [(lambda e, ic=ic, q=q, half=half, pq=pq: e.matmul(
                pq[:, half * 512:(half + 1) * 512],
                tTz.ap()[:, ic * 128:(ic + 1) * 128],
                w2t.ap()[:, ic * HID + q * 1024 + half * 512:
                         ic * HID + q * 1024 + (half + 1) * 512],
                start=(ic == 0), stop=False, skip_group_check=True))
                for ic in range(8)]
            mms.append(lambda e, q=q, half=half, pq=pq: e.matmul(
                pq[:, half * 512:(half + 1) * 512], onesrow.ap()[:],
                c2row.ap()[0:1, q * 1024 + half * 512:
                           q * 1024 + (half + 1) * 512],
                start=False, stop=True, skip_group_check=True))
            sch.group("tensor", mms,
                      reads=("tTz", "onesrow", "c2row") +
                      tuple(f"w2t{ic}" for ic in range(8)),
                      writes=(bq,))
        HG = hgs[h2].ap()
        sch.op("scalar", (lambda e, pq=pq, HG=HG: e.activation(
            out=HG[:], in_=pq[:], func=AF.Gelu)),
            reads=(bq,), writes=(f"hgq{h2}",))
        if q != 3:
            hstats(i, q)

    def hstats(i, q):
        k = i % 2
        h2 = q % 2
        HG = hgs[h2].ap()
        sch.op("vector", (lambda e, HG=HG, k=k, q=q: e.bn_stats(
            out=hstat.ap()[:, k * 48 + q * 12:k * 48 + q * 12 + 6],
            in_=HG[:, 0:512])),
            reads=(f"hgq{h2}",), writes=(f"hstat{k}",))
        sch.op("vector", (lambda e, HG=HG, k=k, q=q: e.bn_stats(
            out=hstat.ap()[:, k * 48 + q * 12 + 6:k * 48 + q * 12 + 12],
            in_=HG[:, 512:1024])),
            reads=(f"hgq{h2}",), writes=(f"hstat{k}",))

    def hTq(i, q):
        h2 = q % 2
        transpose_tile(hgs[h2], f"hgq{h2}", xhts[h2], f"xht{h2}", 8,
                       evac="vector")

    def fc2q(i, q):
        h2 = q % 2
        for half in range(2):
            mms = [(lambda e, c=c, q=q, half=half, h2=h2: e.matmul(
                pA[:, half * 512:(half + 1) * 512],
                xhts[h2].ap()[:, c * 128:(c + 1) * 128],
                w3t.ap()[:, (q * 8 + c) * D + half * 512:
                         (q * 8 + c) * D + (half + 1) * 512],
                start=(q == 0 and c == 0), stop=False,
                skip_group_check=True))
                for c in range(8)]
            sch.group("tensor", mms,
                      reads=(f"xht{h2}",) + tuple(
                          f"w3t{q * 8 + c}" for c in range(8)),
                      writes=("pA",))

    def lnhfin(i):
        k = i % 2
        st = stats2[k].ap()
        sm = smalls2[k].ap()
        sch.op("vector", (lambda e, k=k: e.bn_aggr(
            out=st[:, 0:2], in_=hstat.ap()[:, k * 48:k * 48 + 48])),
            reads=(f"hstat{k}",), writes=(f"st2s{k}",))
        sch.op("vector", (lambda e: e.tensor_scalar(
            out=sm[:, 3:4], in0=st[:, 1:2], scalar1=1.0, scalar2=EPS,
            op0=ALU.mult, op1=ALU.add)), reads=(f"st2s{k}",),
            writes=(f"sm2s{k}",))
        sch.op("vector", (lambda e: e.reciprocal(out=sm[:, 6:7], in_=sm[:, 3:4])),
               reads=(f"sm2s{k}",), writes=(f"sm2s{k}",))
        sch.op("scalar", (lambda e: e.activation(out=sm[:, 4:5], in_=sm[:, 6:7],
                                                 func=AF.Sqrt)),
               reads=(f"sm2s{k}",), writes=(f"sm2s{k}",))
        sch.op("vector", (lambda e: e.tensor_scalar(
            out=acol2.ap()[:, 1:2], in0=sm[:, 3:4], scalar1=sm[:, 4:5],
            scalar2=None, op0=ALU.mult)),
            reads=(f"sm2s{k}",), writes=("acol2",))
        sch.op("vector", (lambda e: e.tensor_scalar(
            out=acol2.ap()[:, 0:1], in0=st[:, 0:1], scalar1=-1.0, scalar2=None,
            op0=ALU.mult)), reads=(f"st2s{k}",), writes=("acol2",))

    def inject(i):
        sch.group("tensor", [(lambda e: e.matmul(
            pD[0:2, 0:128], acol2.ap()[:, 0:2], ident.ap(),
            start=True, stop=True, skip_group_check=True))],
            reads=("acol2", "ident"), writes=("pD",))
        sch.op("vector", (lambda e: e.tensor_copy(out=arows2.ap()[:],
                                                  in_=pD[0:2, 0:128])),
               reads=("pD",), writes=("arows2",))
        for half in range(2):
            sch.group("tensor", [(lambda e, half=half: e.matmul(
                pA[:, half * 512:(half + 1) * 512], arows2.ap()[0:2, :],
                rows2.ap()[0:2, half * 512:(half + 1) * 512],
                start=False, stop=True, skip_group_check=True))],
                reads=("arows2", "rows2"), writes=("pA",))

    def x1xf(i):
        k = i % 2
        st = stats2[k].ap()
        sm = smalls2[k].ap()
        X1 = x1s[0].ap()
        sch.op("vector", (lambda e, k=k: e.scalar_tensor_tensor(
            out=X1[:], in0=pA[:], scalar=sm[:, 4:5], in1=sbuf_s[k].ap()[:],
            op0=ALU.mult, op1=ALU.add)),
            reads=("pA", f"sm2s{k}", f"ss{k}"), writes=("x1s0",))
        DMA("sync", scr_x1[i * 128:(i + 1) * 128, :], X1[:], ("x1s0",),
            (f"scr_x1{i}",))
        sch.op("vector", (lambda e: e.bn_stats(out=st[:, 2:8], in_=X1[:, 0:512])),
               reads=("x1s0",), writes=(f"st2s{k}",))
        sch.op("vector", (lambda e: e.bn_stats(out=st[:, 8:14], in_=X1[:, 512:1024])),
               reads=("x1s0",), writes=(f"st2s{k}",))
        sch.op("vector", (lambda e: e.bn_aggr(out=st[:, 0:2], in_=st[:, 2:14])),
               reads=(f"st2s{k}",), writes=(f"st2s{k}",))
        sch.op("vector", (lambda e: e.tensor_scalar(
            out=sm[:, 7:8], in0=st[:, 1:2], scalar1=1.0, scalar2=EPS,
            op0=ALU.mult, op1=ALU.add)), reads=(f"st2s{k}",), writes=(f"sm2s{k}",))
        sch.op("vector", (lambda e: e.reciprocal(out=sm[:, 8:9], in_=sm[:, 7:8])),
               reads=(f"sm2s{k}",), writes=(f"sm2s{k}",))
        sch.op("scalar", (lambda e: e.activation(out=sm[:, 9:10], in_=sm[:, 8:9],
                                                 func=AF.Sqrt)),
               reads=(f"sm2s{k}",), writes=(f"sm2s{k}",))
        sch.op("vector", (lambda e: e.tensor_scalar(
            out=sm[:, 10:11], in0=st[:, 0:1], scalar1=sm[:, 9:10], scalar2=-1.0,
            op0=ALU.mult, op1=ALU.mult)), reads=(f"st2s{k}", f"sm2s{k}"),
            writes=(f"sm2s{k}",))
        sch.op("scalar", (lambda e, k=k: e.activation(
            out=xfb.ap()[:], in_=X1[:], func=AF.Identity, bias=sm[:, 10:11],
            scale=sm[:, 9:10])), reads=("x1s0", f"sm2s{k}"), writes=("xfb",))

    def xfT_scores(i):
        transpose_tile(xfb, "xfb", xfts[0], "xft0", 8, evac="scalar")
        DMA("sync", scr_xft[i * 128:(i + 1) * 128, :], xfts[0].ap()[:],
            ("xft0",), (f"scr_xft{i}",))
        mms = [(lambda e, c=c: e.matmul(
            pD[:, 0:NEXP], xfts[0].ap()[:, c * 128:(c + 1) * 128],
            signt.ap()[:, c * NEXP:(c + 1) * NEXP],
            start=(c == 0), stop=(c == 7), skip_group_check=True))
            for c in range(8)]
        sch.group("tensor", mms, reads=("xft0", "signt"), writes=("pD",))
        sch.op("vector", (lambda e: e.tensor_copy(out=scm.ap()[:, 0:NEXP],
                                                  in_=pD[:, 0:NEXP])),
               reads=("pD",), writes=("scm",))
        sch.op("vector", (lambda e: e.max(out=scm.ap()[:, NEXP:2 * NEXP],
                                          in_=scm.ap()[:, 0:NEXP])),
               reads=("scm",), writes=("scm",))
        sch.op("vector", (lambda e: e.max_index(out=scmu.ap()[:],
                                                in_max=scm.ap()[:, NEXP:2 * NEXP],
                                                in_values=scm.ap()[:, 0:NEXP])),
               reads=("scm",), writes=("scmu",))
        sch.op("vector", (lambda e: e.tensor_copy(
            out=scm.ap()[:, 2 * NEXP:2 * NEXP + 1], in_=scmu.ap()[:, 0:1])),
            reads=("scmu",), writes=("scm",))
        sch.op("vector", (lambda e, i=i: e.tensor_scalar(
            out=masks.ap()[:, i * NEXP:(i + 1) * NEXP], in0=iota.ap()[:],
            scalar1=scm.ap()[:, 2 * NEXP:2 * NEXP + 1], scalar2=None,
            op0=ALU.is_equal)), reads=("scm", "iota"), writes=(f"masks{i}",))

    _wej = [0]

    def we_load(lo, hi):
        for j in range(lo, hi):
            e_i, dc = j // 8, j % 8
            for half in range(2):
                k = _wej[0] % 3
                ceng = ("vector", "scalar")[_wej[0] % 2]
                _wej[0] += 1
                DMA(("sync", "gpsimd", "scalar")[k], westg[k].ap()[:],
                    i_we[e_i][dc * 128:(dc + 1) * 128,
                              half * 512:(half + 1) * 512],
                    (), (f"westg{k}",))
                if ceng == "vector":
                    sch.op("vector", (lambda e, k=k, j=j, half=half:
                           e.tensor_copy(
                               out=we.ap()[:, j * D + half * 512:
                                           j * D + (half + 1) * 512],
                               in_=westg[k].ap()[:])),
                        reads=(f"westg{k}",), writes=(f"we{j}",))
                else:
                    sch.op("scalar", (lambda e, k=k, j=j, half=half:
                           e.activation(
                               out=we.ap()[:, j * D + half * 512:
                                           j * D + (half + 1) * 512],
                               in_=westg[k].ap()[:], func=AF.Identity)),
                        reads=(f"westg{k}",), writes=(f"we{j}",))

    def pg3_a(i):
        k = i % 2
        XFT = rxft[k].ap()
        X1B = rx1b[k].ap()
        if dbg and i == 0:
            def dump(name, src_ap, shape, dtype, reads):
                d = nc.dram_tensor(name, shape, dtype,
                                   kind="ExternalOutput").ap()
                DMA("sync", d, src_ap, reads, (name,))
            dump("d_ballb", ballb.ap()[:], [NEXP, D], BF16, ("ballb",))
            dump("d_we0", we.ap()[:, 0:8 * D], [128, 8 * D], BF16,
                 tuple(f"we{j}" for j in range(8)))
        DMA("sync", XFT[:], scr_xft[i * 128:(i + 1) * 128, :], (f"scr_xft{i}",),
            (f"rxft{k}",))
        DMA("sync", X1B[:], scr_x1[i * 128:(i + 1) * 128, :], (f"scr_x1{i}",),
            (f"rx1b{k}",))
        if dbg and i == 1:
            def dump(name, src_ap, shape, dtype, reads):
                d = nc.dram_tensor(name, shape, dtype,
                                   kind="ExternalOutput").ap()
                DMA("sync", d, src_ap, reads, (name,))
            dump("d_xftr", rxft[0].ap()[:], [128, D], BF16, ("rxft0",))
            dump("d_x1br", rx1b[0].ap()[:], [128, D], BF16, ("rx1b0",))
        sch.op("vector", (lambda e, i=i: e.tensor_copy(
            out=mbf.ap()[:], in_=masks.ap()[:, i * NEXP:(i + 1) * NEXP])),
            reads=(f"masks{i}",), writes=("mbf",))
        sch.group("tensor", [(lambda e: e.matmul(
            pD[0:NEXP, 128:256], mbf.ap()[:], ident.ap(),
            start=True, stop=True, skip_group_check=True))],
            reads=("mbf", "ident"), writes=("pD",))
        sch.op("vector", (lambda e: e.tensor_copy(
            out=mstt.ap()[:], in_=pD[0:NEXP, 128:256])),
            reads=("pD",), writes=("mstt",))
        for half in range(2):
            sch.group("tensor", [(lambda e, half=half: e.matmul(
                pC[:, half * 512:(half + 1) * 512],
                mstt.ap()[0:NEXP, :],
                ballb.ap()[:, half * 512:(half + 1) * 512],
                start=True, stop=True, skip_group_check=True))],
                reads=("mstt", "ballb"), writes=("pC",))
        ACC = accf[k].ap()
        sch.op("vector", (lambda e: e.tensor_tensor(
            out=ACC[:], in0=pC[:], in1=X1B[:], op=ALU.add)),
            reads=("pC", f"rx1b{k}"), writes=(f"acc{k}",))

    def pg3_b(i):
        k = i % 2
        XFT = rxft[k].ap()
        ACC = accf[k].ap()
        for e_i in range(NEXP):
            pq = (pA, pB)[e_i % 2]
            bq = ("pA", "pB")[e_i % 2]
            for half in range(2):
                mms = [(lambda e, c=c, e_i=e_i, half=half, pq=pq: e.matmul(
                    pq[:, half * 512:(half + 1) * 512],
                    XFT[:, c * 128:(c + 1) * 128],
                    we.ap()[:, (e_i * 8 + c) * D + half * 512:
                            (e_i * 8 + c) * D + (half + 1) * 512],
                    start=(c == 0), stop=(c == 7), skip_group_check=True))
                    for c in range(8)]
                sch.group("tensor", mms,
                          reads=(f"rxft{k}",) + tuple(
                              f"we{e_i * 8 + c}" for c in range(8)),
                          writes=(bq,))
            sch.op("vector", (lambda e, pq=pq, e_i=e_i, i=i:
                   e.scalar_tensor_tensor(
                       out=ACC[:], in0=pq[:],
                       scalar=masks.ap()[:, i * NEXP + e_i:i * NEXP + e_i + 1],
                       in1=ACC[:], op0=ALU.mult, op1=ALU.add)),
                reads=(bq, f"masks{i}", f"acc{k}"), writes=(f"acc{k}",))
        DMA("gpsimd", out[i * 128:(i + 1) * 128, :], ACC[:], (f"acc{k}",),
            (f"out{i}",))

    def alias(newbufs, oldbufs):
        for nb in newbufs:
            s = sch.bufs.setdefault(nb, {"w": [], "r": []})
            for ob in oldbufs:
                os_ = sch.bufs.get(ob)
                if os_:
                    s["w"] = s["w"] + os_["w"] + os_["r"]

    alias(PG1_BUFS, PREP_BUFS)
    pg1_load(0)
    pg1_load(1)
    uchain(0)
    uT_proj(0)
    x2z_v(0)
    x2z_s(0)
    uchain(1)
    pg1_load(2)
    zT(0)
    for i in range(ntiles):
        if 1 <= i and i + 1 < ntiles:
            uchain(i + 1)
        fc1q(i, 0)
        fc1q(i, 1)
        hTq(i, 0)
        fc2q(i, 0)
        if i >= 1:
            xfT_scores(i - 1)
        fc1q(i, 2)
        hTq(i, 1)
        fc2q(i, 1)
        if i + 1 < ntiles:
            uT_proj(i + 1)
        hTq(i, 2)
        if i + 1 < ntiles:
            x2z_v(i + 1)
        fc1q(i, 3)
        fc2q(i, 2)
        hTq(i, 3)
        hstats(i, 3)
        if i + 1 < ntiles:
            x2z_s(i + 1)
        fc2q(i, 3)
        lnhfin(i)
        inject(i)
        if i + 1 < ntiles:
            zT(i + 1)
        x1xf(i)
        if i + 3 < ntiles:
            pg1_load(i + 3)
    xfT_scores(ntiles - 1)
    alias([f"we{j}" for j in range(0, 40)],
          [f"w1t{ic}" for ic in range(8)] +
          [f"w2t{ic}" for ic in range(8)])
    we_load(0, 40)
    alias([f"we{j}" for j in range(40, 64)],
          [f"w3t{hc}" for hc in range(32)])
    we_load(40, 64)
    alias(PG3_BUFS, PG1_BUFS)
    pg3_a(0)
    for i in range(ntiles - 1):
        pg3_a(i + 1)
        pg3_b(i)
    pg3_b(ntiles - 1)

    sch.finish()

    sems = {}
    semlist = []
    for e in ENGINES:
        sems[("e", e)] = nc.alloc_semaphore(name=f"se_{e}")
        semlist.append(sems[("e", e)])
    for key in sch.dcnt:
        sems[key] = nc.alloc_semaphore(name=f"sd_{key[1]}{key[2]}")
        semlist.append(sems[key])

    with nc.Block() as block:
        def runner(eng_name):
            def _run(engobj):
                for item in sch.streams[eng_name]:
                    if item[0] == "w":
                        engobj.wait_ge(sems[item[1]], item[2])
                    else:
                        _, emit, key, amt = item
                        inst = emit(engobj)
                        if key is not None:
                            inst.then_inc(sems[key], amt)
            return _run

        block.sync(runner("sync"))
        block.gpsimd(runner("gpsimd"))
        block.vector(runner("vector"))
        block.scalar(runner("scalar"))
        block.tensor(runner("tensor"))

    for s in semlist:
        nc.release_semaphore(s)
    ctx.close()
    nc._sch = sch
    return nc


_NC_CACHE = {}


def _get_nc(ntiles=NT):
    if ntiles not in _NC_CACHE:
        _NC_CACHE[ntiles] = build_nc(ntiles)
    return _NC_CACHE[ntiles]


def host_layout(inputs):
    f = np.float32

    def a(v):
        return np.ascontiguousarray(np.asarray(v, dtype=f))
    chunks = lambda v, n: np.ascontiguousarray(
        np.asarray(v, dtype=f).reshape(n, 128).T)
    m = {
        "proj_wt": a(np.asarray(inputs["L0_attn_proj_w"]).T),
        "fc1_wt": a(np.asarray(inputs["L0_attn_fc1_w"]).T),
        "fc2_wt": a(np.asarray(inputs["L0_attn_fc2_w"]).T),
        "ffn_W": a(inputs["L0_ffn_W"]),
        "sigst": np.ascontiguousarray(
            np.asarray(inputs["L0_ffn_tile_sigs"], dtype=f).T
            .reshape(8, 128, NEXP).transpose(1, 0, 2).reshape(128, 8 * NEXP)),
        "g1c_i": chunks(inputs["L0_attn_ln1_g"], 8),
        "b1c_i": chunks(inputs["L0_attn_ln1_b"], 8),
        "g2c_i": chunks(inputs["L0_attn_ln2_g"], 8),
        "b2c_i": chunks(inputs["L0_attn_ln2_b"], 8),
        "ghc_i": chunks(inputs["L0_attn_lnh_g"], 32),
        "bhc_i": chunks(inputs["L0_attn_lnh_b"], 32),
        "gm1bc_i": np.ascontiguousarray(np.broadcast_to(
            np.asarray(inputs["L0_attn_gamma1"], dtype=f), (128, D))),
        "gm2bc_i": np.ascontiguousarray(np.broadcast_to(
            np.asarray(inputs["L0_attn_gamma2"], dtype=f), (128, D))),
        "proj_b": a(inputs["L0_attn_proj_b"]),
        "fc1_b": a(inputs["L0_attn_fc1_b"]),
        "fc2_b": a(inputs["L0_attn_fc2_b"]),
        "ffn_b": a(inputs["L0_ffn_b"]),
        "ident": np.eye(128, dtype=ml_dtypes.bfloat16),
        "iota": np.broadcast_to(
            np.arange(NEXP, dtype=f), (128, NEXP)).copy(),
    }
    return m


def kernel(**inputs):
    x = np.ascontiguousarray(np.asarray(inputs["L0_attn_x"], dtype=np.float32))
    B, S, Dm = x.shape
    xf = x.reshape(B * S, Dm)
    nc = _get_nc(NT)
    shared = host_layout(inputs)
    in_maps = []
    for c in range(NCORES):
        m = dict(shared)
        m["x"] = np.ascontiguousarray(
            xf[c * TOK_PER_CORE:(c + 1) * TOK_PER_CORE])
        in_maps.append(m)
    res = run_bass_kernel_spmd(nc, in_maps, core_ids=list(range(NCORES)))
    outs = [res.results[c]["out"] for c in range(NCORES)]
    y = np.concatenate(outs, axis=0).reshape(B, S, Dm).astype(np.float32)
    return y
